# version 16
# baseline (speedup 1.0000x reference)
import numpy as np
import ml_dtypes
import concourse.bass as bass
import concourse.mybir as mybir
from concourse.bass_utils import run_bass_kernel_spmd
from contextlib import ExitStack

F32 = mybir.dt.float32
BF16 = mybir.dt.bfloat16
I32 = mybir.dt.int32
AF = mybir.ActivationFunctionType
ALU = mybir.AluOpType
AX = mybir.AxisListType

DM = 1024
HID = 3072
NH = 16
HD = 64
TCORE = 4096
SLAB = 512
NSLAB = TCORE // SLAB
NCH = SLAB // 128
SEQ_SLABS = 2048 // SLAB
SFF = 1.0 / 127.0
C_DOWN = 0.5 * SFF * SFF
EPS_LN = 1e-5
EPS_DEN = 1e-6

N_SUBLAYERS = 3
DEBUG_TAP = None
ATTN_STAGE = 6


class Builder:

    ENGINES = ("sync", "gpsimd", "tensor", "vector", "scalar")

    def __init__(self, nc, ctx):
        self.nc = nc
        self.ctx = ctx
        self.q = {e: [] for e in self.ENGINES}
        self.sems = {}
        self.cnt = {}
        self.owner = {}
        for name, eng in [("sGPc", "gpsimd"), ("sT", "tensor"),
                          ("sV", "vector"), ("sA", "scalar")]:
            self._new_sem(name, eng)
        self.res = {}

    def _new_sem(self, name, eng):
        self.sems[name] = self.ctx.enter_context(self.nc.semaphore(name))
        self.cnt[name] = 0
        self.owner[name] = eng

    def _r(self, name):
        if name not in self.res:
            self.res[name] = {"ready": {}, "free": {}}
        return self.res[name]

    @staticmethod
    def _merge(dst, src):
        for s, v in src.items():
            if dst.get(s, 0) < v:
                dst[s] = v

    def alias_release(self, dst, srcs):
        d = self._r(dst)
        for s in srcs:
            r = self._r(s)
            self._merge(d["free"], r["free"])
            self._merge(d["free"], r["ready"])

    def _queue(self, eng, fn, reads, writes, sem, inc, extra, accum):
        waits = {}
        for r in reads:
            self._merge(waits, self._r(r)["ready"])
        for r in writes:
            self._merge(waits, self._r(r)["ready"])
            self._merge(waits, self._r(r)["free"])
        for s, v in extra:
            if waits.get(s, 0) < v:
                waits[s] = v
        waits = {s: v for s, v in waits.items()
                 if not (self.owner[s] == eng == "tensor")}
        for s, v in waits.items():
            assert v <= self.cnt[s], f"forward wait {s}={v} > {self.cnt[s]}"
        self.cnt[sem] += inc
        after = self.cnt[sem]
        self.q[eng].append((fn, sorted(waits.items()), (self.sems[sem], inc)))
        for r in reads:
            self._merge(self._r(r)["free"], {sem: after})
        for r in writes:
            if accum:
                self._merge(self._r(r)["ready"], {sem: after})
                self._merge(self._r(r)["free"], {sem: after})
            else:
                self.res[r]["ready"] = {sem: after}
                self.res[r]["free"] = {sem: after}
        return (sem, after)

    def op(self, eng, fn, reads=(), writes=(), sem=None, extra=()):
        if sem is None:
            sem = {"gpsimd": "sGPc", "tensor": "sT",
                   "vector": "sV", "scalar": "sA"}[eng]
        return self._queue(eng, fn, reads, writes, sem, 1, extra, False)

    def dma(self, eng, fn, reads=(), writes=(), sem_key=None, extra=(), accum=False):
        key = sem_key if sem_key is not None else writes[0]
        sem = "d_" + key
        if sem not in self.sems:
            self._new_sem(sem, None)
        return self._queue(eng, fn, reads, writes, sem, 16, extra, accum)

    def mark(self, *sems):
        return [(s, self.cnt[s]) for s in sems]


def build_program():
    nc = bass.Bass("TRN2", target_bir_lowering=False)

    x_in = nc.dram_tensor("x", [TCORE, DM], F32, kind="ExternalInput").ap()
    w_qkv = nc.dram_tensor("w_qkv", [DM, 3 * DM], F32, kind="ExternalInput").ap()
    w_gate = nc.dram_tensor("w_gate", [DM, DM], F32, kind="ExternalInput").ap()
    w_out = nc.dram_tensor("w_out", [DM, DM], F32, kind="ExternalInput").ap()
    wg_int = nc.dram_tensor("wg_int", [HID, DM], I32, kind="ExternalInput").ap()
    wu_int = nc.dram_tensor("wu_int", [HID, DM], I32, kind="ExternalInput").ap()
    wd_int = nc.dram_tensor("wd_int", [DM, HID], I32, kind="ExternalInput").ap()
    ident_in = nc.dram_tensor("ident", [128, 128], BF16, kind="ExternalInput").ap()
    mask_in = nc.dram_tensor("mask4", [128, 512], F32, kind="ExternalInput").ap()
    consts_in = nc.dram_tensor("consts", [128, 4], F32, kind="ExternalInput").ap()
    zeros_in = nc.dram_tensor("zeros", [128, 512], BF16, kind="ExternalInput").ap()
    y_out = nc.dram_tensor("y", [TCORE, DM], F32, kind="ExternalOutput").ap()

    wg_bf = nc.dram_tensor("wg_bf", [HID, DM], BF16).ap()
    wu_bf = nc.dram_tensor("wu_bf", [HID, DM], BF16).ap()
    wd_bf = nc.dram_tensor("wd_bf", [DM, HID], BF16).ap()
    wgT_blk = nc.dram_tensor("wgT_blk", [24, 128, 8, 128], BF16).ap()
    wuT_blk = nc.dram_tensor("wuT_blk", [24, 128, 8, 128], BF16).ap()
    wdT_blk = nc.dram_tensor("wdT_blk", [8, 128, 24, 128], BF16).ap()
    wgate_blk = nc.dram_tensor("wgate_blk", [8, 128, 8, 128], BF16).ap()
    wout_blk = nc.dram_tensor("wout_blk", [8, 128, 8, 128], BF16).ap()

    ctx = ExitStack()
    e = ctx.enter_context

    wqkv_sb = e(nc.sbuf_tensor("wqkv_sb", [128, 8 * 3072], BF16))
    xa_sb = e(nc.sbuf_tensor("xa_sb", [128, NCH * DM], F32))
    xb_sb = e(nc.sbuf_tensor("xb_sb", [128, NCH * DM], F32))
    x1_sb = e(nc.sbuf_tensor("x1_sb", [128, NCH * DM], F32))
    fm_sc = e(nc.sbuf_tensor("fm_sc", [128, 8 * SLAB], BF16))
    big_sc = e(nc.sbuf_tensor("big_sc", [128, 24 * SLAB], BF16))
    wg_buf = e(nc.sbuf_tensor("wg_buf", [128, 3 * 1024], BF16))
    wu_buf = e(nc.sbuf_tensor("wu_buf", [128, 3 * 1024], BF16))
    wd_buf = e(nc.sbuf_tensor("wd_buf", [128, 2 * 3072], BF16))
    wga_buf = e(nc.sbuf_tensor("wga_buf", [128, 2 * 1024], BF16))
    wou_buf = e(nc.sbuf_tensor("wou_buf", [128, 2 * 1024], BF16))
    qfm_sb = e(nc.sbuf_tensor("qfm_sb", [128, 8 * SLAB], BF16))
    kfm_sb = e(nc.sbuf_tensor("kfm_sb", [128, 8 * SLAB], BF16))
    ktm_sb = e(nc.sbuf_tensor("ktm_sb", [128, 2 * DM], BF16))
    vaug_sb = e(nc.sbuf_tensor("vaug_sb", [128, 2 * NH * (HD + 1)], BF16))
    gtm_sb = e(nc.sbuf_tensor("gtm_sb", [128, 2 * DM], BF16))
    state_sb = e(nc.sbuf_tensor("state_sb", [128, 8 * 128], BF16))
    st_sb = e(nc.sbuf_tensor("st_sb", [128, 2 * SLAB], BF16))
    ytm_sb = e(nc.sbuf_tensor("ytm_sb", [128, 2 * DM], BF16))
    sg_sb = e(nc.sbuf_tensor("sg_sb", [128, 2 * SLAB], BF16))
    r_sb = e(nc.sbuf_tensor("r_sb", [128, SLAB], F32))
    d_sb = e(nc.sbuf_tensor("d_sb", [128, SLAB], F32))
    e_sb = e(nc.sbuf_tensor("e_sb", [128, SLAB], F32))
    stats_sb = e(nc.sbuf_tensor("stats_sb", [128, 64], F32))
    den_sb = e(nc.sbuf_tensor("den_sb", [128, 8], F32))
    consts_sb = e(nc.sbuf_tensor("consts_sb", [128, 4], F32))
    mask_sb = e(nc.sbuf_tensor("mask_sb", [128, 512], F32))
    ident_sb = e(nc.sbuf_tensor("ident_sb", [128, 128], BF16))
    zeros_sb = e(nc.sbuf_tensor("zeros_sb", [128, 512], BF16))

    pS = e(nc.psum_tensor("pS", [128, 1024], F32))
    pA = e(nc.psum_tensor("pA", [128, 512], F32))
    pB = e(nc.psum_tensor("pB", [128, 512], F32))
    pC = e(nc.psum_tensor("pC", [128, 512], F32))
    pT = e(nc.psum_tensor("pT", [128, 1024], BF16))
    pT2 = e(nc.psum_tensor("pT2", [128, 1024], BF16))
    pO = e(nc.psum_tensor("pO", [128, 512], F32))

    import sys as _sys
    print("SBUF remaining after allocs:", nc.sbuf_bytes_remaining, file=_sys.stderr)
    b = Builder(nc, ctx)

    BANK3 = ((pA, "pA"), (pB, "pB"), (pO, "pO"))

    _ptst = {"i": 0}

    def pt_next():
        _ptst["i"] ^= 1
        return (pT, "pT") if _ptst["i"] else (pT2, "pT2")

    HH = lambda m: big_sc[:, m * 512:(m + 1) * 512]
    OFM = big_sc[:, 0:4096]
    OFIN = big_sc[:, 4096:8192]
    GFM = big_sc[:, 8192:12288]

    def _stats(par):
        o = par * 32
        return dict(SUMS=stats_sb[:, o:o + 4], SUMSQ=stats_sb[:, o + 4:o + 8],
                    MEAN=stats_sb[:, o + 8:o + 12], MSQ=stats_sb[:, o + 12:o + 16],
                    VAR=stats_sb[:, o + 16:o + 20], LNV=stats_sb[:, o + 20:o + 24],
                    RSTD=stats_sb[:, o + 24:o + 28], NB=stats_sb[:, o + 28:o + 32])
    EPSC = consts_sb[:, 0:1]
    NEGH = consts_sb[:, 1:2]
    SFFC = consts_sb[:, 2:3]

    b.dma("sync", lambda s: s.dma_start(out=consts_sb[:], in_=consts_in), writes=["consts"])
    b.dma("sync", lambda s: s.dma_start(out=zeros_sb[:], in_=zeros_in), writes=["zeros"])
    b.op("gpsimd", lambda g: g.memset(vaug_sb[:], 1.0), writes=["vaug0", "vaug1"])
    b.dma("sync", lambda s: s.dma_start(out=ident_sb[:], in_=ident_in), writes=["ident"])
    b.dma("sync", lambda s: s.dma_start(out=mask_sb[:], in_=mask_in), writes=["mask"])

    def x_load(buf, s):
        src = x_in[s * SLAB:(s + 1) * SLAB].rearrange("(c p) f -> p c f", p=128)
        return lambda g: g.dma_start(out=buf[:], in_=src)
    b.dma("sync", x_load(xa_sb, 0), writes=["xa"])

    for k in range(8):
        b.dma("gpsimd", lambda g, k=k: g.dma_start(
            out=wg_bf[:, k * 128:(k + 1) * 128], in_=wg_int[:, k * 128:(k + 1) * 128]),
            writes=[f"wg_bf{k}"])
    for k in range(8):
        b.dma("gpsimd", lambda g, k=k: g.dma_start(
            out=wu_bf[:, k * 128:(k + 1) * 128], in_=wu_int[:, k * 128:(k + 1) * 128]),
            writes=[f"wu_bf{k}"])
    b.dma("gpsimd", lambda g: g.dma_start(out=wd_bf, in_=wd_int), writes=["wd_bf"])
    wga_src = w_gate.rearrange("(kd p) g -> p kd g", p=128)
    wou_src = w_out.rearrange("(kc p) g -> p kc g", p=128)
    for gc in range(8):
        b.dma("gpsimd", lambda g, gc=gc: g.dma_start(
            out=wgate_blk[gc], in_=wga_src[:, :, gc * 128:(gc + 1) * 128]),
            writes=["wgate_blk"])
    for m in range(8):
        b.dma("gpsimd", lambda g, m=m: g.dma_start(
            out=wout_blk[m], in_=wou_src[:, :, m * 128:(m + 1) * 128]),
            writes=["wout_blk"])
    for kd in range(8):
        b.dma("gpsimd", lambda g, kd=kd: g.dma_start(
            out=wqkv_sb[:, kd * 3072:(kd + 1) * 3072],
            in_=w_qkv[kd * 128:(kd + 1) * 128, :]), writes=["wqkv"])

    for wsrc, wdst, wres, dres in ((wg_bf, wgT_blk, "wg_bf", "wgT_blk"),
                                   (wu_bf, wuT_blk, "wu_bf", "wuT_blk")):
        for k in range(8):
            slot = (k % 2) * 3072
            buf = big_sc[:, slot:slot + 3072]
            b.dma("sync", lambda s, wsrc=wsrc, k=k, buf=buf: s.dma_start(
                out=buf, in_=wsrc[:, k * 128:(k + 1) * 128], transpose=True),
                reads=[f"{wres}{k}"], writes=[f"ph0_{k % 2}"])
            b.dma("scalar", lambda s, wdst=wdst, k=k, buf=buf: s.dma_start(
                out=wdst[:, :, k].rearrange("m p f -> p m f"), in_=buf),
                reads=[f"ph0_{k % 2}"], writes=[dres], sem_key=f"{dres}{k % 2}", accum=True)
    for j in range(24):
        slot = 6144 + (j % 2) * 1024
        buf = big_sc[:, slot:slot + 1024]
        b.dma("sync", lambda s, j=j, buf=buf: s.dma_start(
            out=buf, in_=wd_bf[:, j * 128:(j + 1) * 128], transpose=True),
            reads=["wd_bf"], writes=[f"ph0d_{j % 2}"])
        b.dma("scalar", lambda s, j=j, buf=buf: s.dma_start(
            out=wdT_blk[:, :, j].rearrange("i p f -> p i f"), in_=buf),
            reads=[f"ph0d_{j % 2}"], writes=["wdT_blk"], sem_key=f"wdT_blk{j % 2}", accum=True)
    b.alias_release("big_sc_init", ["ph0_0", "ph0_1", "ph0d_0", "ph0d_1"])

    pS_guard = {"last_update": None, "last_evac": None}

    def emit_ln_stats(xbuf, xres, par, have_sums=False):
        S = _stats(par)
        p = f"p{par}"
        for c in range(NCH):
            xc = xbuf[:, c * DM:(c + 1) * DM]
            b.op("scalar", lambda s, xc=xc, c=c, S=S: s.activation(
                ktm_sb[:, 0:DM], xc, AF.Square, accum_out=S["SUMSQ"][:, c:c + 1]),
                reads=[xres], writes=["sq", "st_sums" + p])
            if not have_sums:
                b.op("vector", lambda v, xc=xc, c=c, S=S: v.reduce_sum(
                    S["SUMS"][:, c:c + 1], xc, axis=AX.X),
                    reads=[xres], writes=["st_sums" + p])
        b.op("vector", lambda v, S=S: v.tensor_scalar(S["MEAN"], S["SUMS"], 1.0 / DM, None, ALU.mult),
             reads=["st_sums" + p], writes=["st_mean" + p])
        b.op("vector", lambda v, S=S: v.tensor_mul(S["MSQ"], S["MEAN"], S["MEAN"]),
             reads=["st_mean" + p], writes=["st_msq" + p])
        b.op("vector", lambda v, S=S: v.scalar_tensor_tensor(
            S["VAR"], S["SUMSQ"], 1.0 / DM, S["MSQ"], ALU.mult, ALU.subtract),
            reads=["st_sums" + p, "st_msq" + p], writes=["st_var" + p])
        b.op("scalar", lambda s, S=S: s.activation(S["LNV"], S["VAR"], AF.Ln, bias=EPSC, scale=1.0),
             reads=["st_var" + p, "consts"], writes=["st_lnv" + p])
        b.op("scalar", lambda s, S=S: s.activation(S["RSTD"], S["LNV"], AF.Exp, bias=0.0, scale=NEGH),
             reads=["st_lnv" + p, "consts"], writes=["st_rstd" + p])
        b.op("vector", lambda v, S=S: v.scalar_tensor_tensor(
            S["NB"], S["MEAN"], -1.0, S["RSTD"], ALU.mult, ALU.mult),
            reads=["st_mean" + p, "st_rstd" + p], writes=["st_nb" + p])

    def emit_ln_apply(xbuf, xres, par):
        S = _stats(par)
        p = f"p{par}"
        for c in range(NCH):
            xc = xbuf[:, c * DM:(c + 1) * DM]
            yslot = ytm_sb[:, (c % 2) * DM:((c % 2) + 1) * DM]
            b.op("scalar", lambda s, xc=xc, c=c, yslot=yslot, S=S: s.activation(
                yslot, xc, AF.Identity, bias=S["NB"][:, c:c + 1], scale=S["RSTD"][:, c:c + 1]),
                reads=[xres, "st_rstd" + p, "st_nb" + p], writes=[f"ytm{c % 2}"])

            ptile, pres = pt_next()

            def transp(t, c=c, yslot=yslot, ptile=ptile):
                for i in range(8):
                    ins = t.transpose(ptile[:, i * 128:(i + 1) * 128],
                                      yslot[:, i * 128:(i + 1) * 128], ident_sb[:])
                return ins
            b.op("tensor", transp, reads=[f"ytm{c % 2}", "ident"], writes=[pres])
            dst = fm_sc.ap().rearrange("p (k cf) -> p k cf", k=8)[:, :, c * 128:(c + 1) * 128]
            src = ptile.ap().rearrange("p (k f) -> p k f", k=8)
            b.op("scalar", lambda s, dst=dst, src=src: s.copy(dst, src),
                 reads=[pres], writes=["fm"])

    def emit_ffn(xbuf, xres, outbuf, outres, sub):
        def wg_dma(m):
            b.dma("sync", lambda s, m=m: s.dma_start(
                out=wg_buf[:, (m % 3) * 1024:((m % 3) + 1) * 1024],
                in_=wgT_blk[m]),
                reads=["wgT_blk"], writes=[f"wg{m % 3}"])
            b.dma("sync", lambda s, m=m: s.dma_start(
                out=wu_buf[:, (m % 3) * 1024:((m % 3) + 1) * 1024],
                in_=wuT_blk[m]),
                reads=["wuT_blk"], writes=[f"wu{m % 3}"])
        wg_dma(0)
        wg_dma(1)
        for m in range(24):
            if m + 2 < 24:
                wg_dma(m + 2)
            if m < 2:
                b.dma("sync", lambda s, m=m: s.dma_start(
                    out=wd_buf[:, (m % 2) * 3072:((m % 2) + 1) * 3072],
                    in_=wdT_blk[m]),
                    reads=["wdT_blk"], writes=[f"wd{m % 2}"])
            bank, bres = BANK3[m % 3]
            wslot = wg_buf[:, (m % 3) * 1024:((m % 3) + 1) * 1024]

            def gmm(t, bank=bank, wslot=wslot):
                for k in range(8):
                    ins = t.matmul(bank[:], wslot[:, k * 128:(k + 1) * 128],
                                   fm_sc[:, k * 512:(k + 1) * 512],
                                   start=(k == 0), stop=(k == 7))
                return ins
            b.op("tensor", gmm, reads=[f"wg{m % 3}", "fm"], writes=[bres])
            if DEBUG_TAP == "g0" and m == 0 and sub == 0:
                b.op("scalar", lambda sc, bank=bank: sc.copy(xb_sb[:, 0:512], bank[:]),
                     reads=[bres], writes=["xb"])
                b.op("vector", lambda v, wslot=wslot: v.tensor_copy(
                    xb_sb[:, 1024:2048], wslot), reads=[f"wg{m % 2}"], writes=["xb"])
                b.op("vector", lambda v: v.tensor_copy(
                    xb_sb[:, 2048:2560], fm_sc[:, 0:512]),
                    reads=["fm"], writes=["xb"])
            uslot = wu_buf[:, (m % 3) * 1024:((m % 3) + 1) * 1024]

            def umm(t, uslot=uslot):
                for k in range(8):
                    ins = t.matmul(pC[:], uslot[:, k * 128:(k + 1) * 128],
                                   fm_sc[:, k * 512:(k + 1) * 512],
                                   start=(k == 0), stop=(k == 7))
                return ins
            b.op("tensor", umm, reads=[f"wu{m % 3}", "fm"], writes=["pC"])
            sgslot = sg_sb[:, (m % 2) * 512:((m % 2) + 1) * 512]
            b.op("scalar", lambda s, bank=bank, sgslot=sgslot: s.activation(
                sgslot, bank[:], AF.Silu, scale=SFFC),
                reads=[bres, "consts"], writes=[f"sg{m % 2}"])
            b.op("vector", lambda v, m=m, sgslot=sgslot: v.tensor_tensor(
                HH(m), sgslot, pC[:], ALU.mult),
                reads=[f"sg{m % 2}", "pC"], writes=["hh"])
        for i in range(8):
            bank, bres = BANK3[i % 3]
            wslot = wd_buf[:, (i % 2) * 3072:((i % 2) + 1) * 3072]

            def dmm(t, bank=bank, wslot=wslot):
                for j in range(24):
                    ins = t.matmul(bank[:], wslot[:, j * 128:(j + 1) * 128], HH(j),
                                   start=(j == 0), stop=(j == 23))
                return ins
            b.op("tensor", dmm, reads=[f"wd{i % 2}", "hh"], writes=[bres])
            if i + 2 < 8:
                b.dma("sync", lambda s, i=i: s.dma_start(
                    out=wd_buf[:, (i % 2) * 3072:((i % 2) + 1) * 3072],
                    in_=wdT_blk[i + 2]),
                    reads=["wdT_blk"], writes=[f"wd{i % 2}"])
            b.op("scalar", lambda s, bank=bank, i=i: s.copy(
                fm_sc[:, i * 512:(i + 1) * 512], bank[:]),
                reads=[bres], writes=["fm"])
        for c in range(NCH):
            ptile, pres = pt_next()

            def dtr(t, c=c, ptile=ptile):
                for i in range(8):
                    ins = t.transpose(ptile[:, i * 128:(i + 1) * 128],
                                      fm_sc[:, i * 512 + c * 128:i * 512 + (c + 1) * 128],
                                      ident_sb[:])
                return ins
            b.op("tensor", dtr, reads=["fm", "ident"], writes=[pres])
            if sub == 0:
                b.op("vector", lambda v, c=c, ptile=ptile: v.scalar_tensor_tensor(
                    outbuf[:, c * DM:(c + 1) * DM], ptile[:], C_DOWN,
                    xbuf[:, c * DM:(c + 1) * DM], ALU.mult, ALU.add,
                    accum_out=_stats(0)["SUMS"][:, c:c + 1]),
                    reads=[pres, xres], writes=[outres, "st_sumsp0"])
            else:
                b.op("vector", lambda v, c=c, ptile=ptile: v.scalar_tensor_tensor(
                    outbuf[:, c * DM:(c + 1) * DM], ptile[:], C_DOWN,
                    xbuf[:, c * DM:(c + 1) * DM], ALU.mult, ALU.add),
                    reads=[pres, xres], writes=[outres])

    def emit_attn(xbuf, xres, outbuf, outres, slab):
        seq_start = (slab % SEQ_SLABS == 0)
        def wga_dma(gc):
            b.dma("sync", lambda s, gc=gc: s.dma_start(
                out=wga_buf[:, (gc % 2) * 1024:((gc % 2) + 1) * 1024],
                in_=wgate_blk[gc]),
                reads=["wgate_blk"], writes=[f"wga{gc % 2}"])
        wga_dma(0)
        wga_dma(1)
        for which, dst, dres, off in (("q", qfm_sb, "qfm", 0), ("k", kfm_sb, "kfm", 1024)):
            for qc in range(8):
                bank, bres = BANK3[qc % 3]

                def pmm(t, bank=bank, qc=qc, off=off):
                    for k in range(8):
                        ins = t.matmul(bank[:],
                                       wqkv_sb[:, k * 3072 + off + qc * 128: k * 3072 + off + (qc + 1) * 128],
                                       fm_sc[:, k * 512:(k + 1) * 512],
                                       start=(k == 0), stop=(k == 7))
                    return ins
                b.op("tensor", pmm, reads=["wqkv", "fm"], writes=[bres])
                b.op("scalar", lambda s, bank=bank: s.activation(r_sb[:], bank[:], AF.Relu),
                     reads=[bres], writes=["r"])
                b.op("vector", lambda v, bank=bank: v.tensor_tensor(d_sb[:], bank[:], r_sb[:], ALU.subtract),
                     reads=[bres, "r"], writes=["d"])
                b.op("scalar", lambda s: s.activation(e_sb[:], d_sb[:], AF.Exp),
                     reads=["d"], writes=["eexp"])
                b.op("vector", lambda v, dst=dst, qc=qc: v.tensor_tensor(
                    dst[:, qc * 512:(qc + 1) * 512], e_sb[:], r_sb[:], ALU.add),
                    reads=["eexp", "r"], writes=[dres])
        if ATTN_STAGE < 1:
            return
        for gc in range(8):
            bank, bres = BANK3[gc % 3]

            def gamm(t, bank=bank, gc=gc):
                for k in range(8):
                    ins = t.matmul(bank[:],
                                   wga_buf[:, (gc % 2) * 1024 + k * 128:(gc % 2) * 1024 + (k + 1) * 128],
                                   fm_sc[:, k * 512:(k + 1) * 512],
                                   start=(k == 0), stop=(k == 7))
                return ins
            b.op("tensor", gamm, reads=[f"wga{gc % 2}", "fm"], writes=[bres])
            if gc + 2 < 8:
                wga_dma(gc + 2)
            b.op("scalar", lambda s, bank=bank, gc=gc: s.copy(
                GFM[:, gc * 512:(gc + 1) * 512], bank[:]),
                reads=[bres], writes=["gfm"])

        if ATTN_STAGE < 2:
            return

        def emit_prep(c):
            cpar = c % 2
            ptile, pres = pt_next()

            def ktr(t, c=c, ptile=ptile):
                for qc in range(8):
                    ins = t.transpose(ptile[:, qc * 128:(qc + 1) * 128],
                                      kfm_sb[:, qc * 512 + c * 128:qc * 512 + (c + 1) * 128],
                                      ident_sb[:])
                return ins
            b.op("tensor", ktr, reads=["kfm", "ident"], writes=[pres])
            b.op("scalar", lambda s, cpar=cpar, ptile=ptile: s.copy(
                ktm_sb[:, cpar * DM:(cpar + 1) * DM], ptile[:]),
                reads=[pres], writes=[f"ktm{cpar}"])

            def vtmm0(t, c=c):
                for k in range(8):
                    ins = t.matmul(pC[:],
                                   fm_sc[:, k * 512 + c * 128:k * 512 + (c + 1) * 128],
                                   wqkv_sb[:, k * 3072 + 2048:k * 3072 + 2048 + 512],
                                   start=(k == 0), stop=(k == 7))
                return ins
            b.op("tensor", vtmm0, reads=["wqkv", "fm"], writes=["pC"])
            vdst = vaug_sb.ap().rearrange("p (s h g) -> p s h g", s=2, h=NH)[
                :, cpar, 0:8, 0:HD]
            vsrc = pC.ap().rearrange("p (h g) -> p h g", h=8)
            b.op("scalar", lambda s, vdst=vdst, vsrc=vsrc: s.copy(vdst, vsrc),
                 reads=["pC"], writes=[f"vaug{cpar}"])
            ptile, pres = pt_next()

            def gtr(t, c=c, ptile=ptile):
                for i in range(8):
                    ins = t.transpose(ptile[:, i * 128:(i + 1) * 128],
                                      GFM[:, i * 512 + c * 128:i * 512 + (c + 1) * 128],
                                      ident_sb[:])
                return ins
            b.op("tensor", gtr, reads=["gfm", "ident"], writes=[pres])
            b.op("scalar", lambda s, cpar=cpar, ptile=ptile: s.activation(
                gtm_sb[:, cpar * DM:(cpar + 1) * DM], ptile[:], AF.Sigmoid),
                reads=[pres], writes=[f"gtm{cpar}"])

        def emit_vtm1(c):
            cpar = c % 2

            def vtmm1(t, c=c):
                for k in range(8):
                    ins = t.matmul(pO[:],
                                   fm_sc[:, k * 512 + c * 128:k * 512 + (c + 1) * 128],
                                   wqkv_sb[:, k * 3072 + 2560:k * 3072 + 2560 + 512],
                                   start=(k == 0), stop=(k == 7))
                return ins
            b.op("tensor", vtmm1, reads=["wqkv", "fm"], writes=["pO"])
            vdst = vaug_sb.ap().rearrange("p (s h g) -> p s h g", s=2, h=NH)[
                :, cpar, 8:16, 0:HD]
            vsrc = pO.ap().rearrange("p (h g) -> p h g", h=8)
            b.op("vector", lambda v, vdst=vdst, vsrc=vsrc: v.tensor_copy(vdst, vsrc),
                 reads=["pO"], writes=[f"vaug{cpar}"])

        HEAD_GROUPS = ([0, 2, 4, 6], [8, 10, 12, 14], [1, 3, 5, 7], [9, 11, 13, 15])
        emit_prep(0)
        emit_vtm1(0)
        for c in range(NCH):
            cpar = c % 2
            first = seq_start and c == 0
            if ATTN_STAGE < 3:
                continue
            if ATTN_STAGE >= 5 and not first:
                extra = [pS_guard["last_update"]] if pS_guard["last_update"] else []
                ev = b.op("scalar", lambda s: s.copy(state_sb[:], pS[:]),
                          writes=["state"], extra=extra)
                pS_guard["last_evac"] = ev

            def emit_st(g):
                gpar = g % 2
                heads = HEAD_GROUPS[g]
                bank, bres = (pA, "pA") if gpar == 0 else (pB, "pB")

                def stmm(t, c=c, bank=bank, heads=heads):
                    t.matmul(bank[:, 0:128], zeros_sb[:, 0:128], zeros_sb[:, 0:128],
                             start=True, stop=True)
                    for jj, h in enumerate(heads):
                        pb = (h % 2) * 64
                        qc = h // 2
                        ins = t.matmul(bank[:, jj * 128:(jj + 1) * 128],
                                       kfm_sb[pb:pb + 64, qc * 512 + c * 128:qc * 512 + (c + 1) * 128],
                                       qfm_sb[pb:pb + 64, qc * 512 + c * 128:qc * 512 + (c + 1) * 128],
                                       start=True, stop=True)
                    return ins
                b.op("tensor", stmm, reads=["kfm", "qfm", "zeros"], writes=[bres])
                b.op("vector", lambda v, bank=bank, gpar=gpar: v.tensor_tensor(
                    st_sb[:, gpar * 512:(gpar + 1) * 512], bank[:], mask_sb[:], ALU.mult),
                    reads=[bres, "mask"], writes=[f"st{gpar}"])

            def emit_o(g):
                gpar = g % 2
                heads = HEAD_GROUPS[g]
                omm_first = first or ATTN_STAGE < 5

                def omm(t, c=c, gpar=gpar, first=omm_first, heads=heads):
                    for jj, h in enumerate(heads):
                        pb = (h % 2) * 64
                        qc = h // 2
                        ins = t.matmul(pO[:, jj * 65:jj * 65 + 65],
                                       st_sb[:, gpar * 512 + jj * 128:gpar * 512 + (jj + 1) * 128],
                                       vaug_sb[:, (c % 2) * NH * 65 + h * 65:(c % 2) * NH * 65 + (h + 1) * 65],
                                       start=True, stop=first)
                        if not first:
                            ins = t.matmul(pO[:, jj * 65:jj * 65 + 65],
                                           qfm_sb[pb:pb + 64, qc * 512 + c * 128:qc * 512 + (c + 1) * 128],
                                           state_sb[pb:pb + 64, (h // 2) * 128:(h // 2) * 128 + 65],
                                           start=False, stop=True)
                    return ins
                o_reads = [f"st{gpar}", f"vaug{cpar}", "qfm"] + ([] if omm_first else ["state"])
                b.op("tensor", omm, reads=o_reads, writes=["pO"])
                dencol = pO[:, 0:260].rearrange("p (j g) -> p j g", g=65)[:, :, 64:65]
                b.op("vector", lambda v, dencol=dencol: v.tensor_scalar(
                    den_sb[:, 0:4].rearrange("p (j o) -> p j o", j=4), dencol,
                    EPS_DEN, None, ALU.add),
                    reads=["pO"], writes=["den"])
                b.op("vector", lambda v: v.reciprocal(den_sb[:, 4:8], den_sb[:, 0:4]),
                     reads=["den"], writes=["recip"])

                def epi(v, c=c, cpar=cpar, heads=heads):
                    for jj, h in enumerate(heads):
                        ins = v.scalar_tensor_tensor(
                            OFIN[:, c * DM + h * HD:c * DM + (h + 1) * HD],
                            pO[:, jj * 65:jj * 65 + 64], den_sb[:, 4 + jj:5 + jj],
                            gtm_sb[:, cpar * DM + h * HD:cpar * DM + (h + 1) * HD],
                            ALU.mult, ALU.mult)
                    return ins
                b.op("vector", epi, reads=["pO", "recip", f"gtm{cpar}"], writes=["ofin"])

            if ATTN_STAGE < 4:
                for g in range(4):
                    emit_st(g)
            else:
                emit_st(0)
                emit_st(1)
                emit_o(0)
                emit_st(2)
                emit_o(1)
                emit_st(3)
                if c + 1 < NCH:
                    emit_prep(c + 1)
                emit_o(2)
                emit_o(3)
                if c + 1 < NCH:
                    emit_vtm1(c + 1)
            if ATTN_STAGE < 5:
                continue
            if first:
                def zmm(t):
                    t.matmul(pS[:, 0:512], zeros_sb[:, 0:128], zeros_sb[:, 0:512],
                             start=True, stop=True)
                    return t.matmul(pS[:, 512:1024], zeros_sb[:, 0:128], zeros_sb[:, 0:512],
                                    start=True, stop=True)
                extra = [pS_guard["last_evac"]] if pS_guard["last_evac"] else []
                b.op("tensor", zmm, reads=["zeros"], extra=extra)

            def upd(t, c=c, cpar=cpar):
                for h in range(NH):
                    rb = (h % 2) * 64
                    tp = (0, 64) if rb == 64 else None
                    ins = t.matmul(pS[rb:rb + 64, (h // 2) * 128:(h // 2) * 128 + 65],
                                   ktm_sb[:, cpar * DM + h * HD:cpar * DM + (h + 1) * HD],
                                   vaug_sb[:, cpar * NH * 65 + h * 65:cpar * NH * 65 + (h + 1) * 65],
                                   start=False, stop=True, tile_position=tp)
                return ins
            extra = [pS_guard["last_evac"]] if pS_guard["last_evac"] else []
            up = b.op("tensor", upd, reads=[f"ktm{cpar}", f"vaug{cpar}"], extra=extra)
            pS_guard["last_update"] = up
            ptile, pres = pt_next()

            def otr(t, c=c, ptile=ptile):
                for i in range(8):
                    ins = t.transpose(ptile[:, i * 128:(i + 1) * 128],
                                      OFIN[:, c * DM + i * 128:c * DM + (i + 1) * 128],
                                      ident_sb[:])
                return ins
            b.op("tensor", otr, reads=["ofin", "ident"], writes=[pres])
            odst = OFM.rearrange("p (k cf) -> p k cf", k=8)[:, :, c * 128:(c + 1) * 128]
            osrc = ptile.ap().rearrange("p (k f) -> p k f", k=8)
            b.op("scalar", lambda s, odst=odst, osrc=osrc: s.copy(odst, osrc),
                 reads=[pres], writes=["ofm"])

        if ATTN_STAGE < 6:
            return
        def wou_dma(m):
            b.dma("sync", lambda s, m=m: s.dma_start(
                out=wou_buf[:, (m % 2) * 1024:((m % 2) + 1) * 1024],
                in_=wout_blk[m]),
                reads=["wout_blk"], writes=[f"wou{m % 2}"])
        wou_dma(0)
        wou_dma(1)
        for m in range(8):
            bank, bres = BANK3[m % 3]

            def opmm(t, bank=bank, m=m):
                for kc in range(8):
                    ins = t.matmul(bank[:],
                                   wou_buf[:, (m % 2) * 1024 + kc * 128:(m % 2) * 1024 + (kc + 1) * 128],
                                   OFM[:, kc * 512:(kc + 1) * 512],
                                   start=(kc == 0), stop=(kc == 7))
                return ins
            b.op("tensor", opmm, reads=[f"wou{m % 2}", "ofm"], writes=[bres])
            if m + 2 < 8:
                wou_dma(m + 2)
            b.op("scalar", lambda s, bank=bank, m=m: s.copy(
                fm_sc[:, m * 512:(m + 1) * 512], bank[:]),
                reads=[bres], writes=["fm"])
        for c in range(NCH):
            ptile, pres = pt_next()

            def atr(t, c=c, ptile=ptile):
                for i in range(8):
                    ins = t.transpose(ptile[:, i * 128:(i + 1) * 128],
                                      fm_sc[:, i * 512 + c * 128:i * 512 + (c + 1) * 128],
                                      ident_sb[:])
                return ins
            b.op("tensor", atr, reads=["fm", "ident"], writes=[pres])
            b.op("vector", lambda v, c=c, ptile=ptile: v.scalar_tensor_tensor(
                outbuf[:, c * DM:(c + 1) * DM], ptile[:], 1.0,
                xbuf[:, c * DM:(c + 1) * DM], ALU.mult, ALU.add,
                accum_out=_stats(0)["SUMS"][:, c:c + 1]),
                reads=[pres, xres], writes=[outres, "st_sumsp0"])

    if DEBUG_TAP == "wgt_disabled":
        for m in range(24):
            dst = y_out[m * 128:(m + 1) * 128, :].rearrange("p (k f) -> p k f", f=128)
            b.dma("gpsimd", lambda g, dst=dst, m=m: g.dma_start(
                out=dst, in_=wgT_blk[m]),
                reads=["wgT_blk"], sem_key="y", accum=True)
        for m in range(8):
            dst = y_out[3072 + m * 128:3072 + (m + 1) * 128, :].rearrange("p (k f) -> p k f", f=128)
            b.dma("gpsimd", lambda g, dst=dst, m=m: g.dma_start(
                out=dst, in_=wgate_blk[m].rearrange("k p f -> p k f")),
                reads=["wgate_blk"], sem_key="y", accum=True)
        total_y = b.cnt["d_y"]
        b.q["gpsimd"].append((lambda g: g.wait_ge(b.sems["d_y"], total_y), [], (None, 0)))
        with nc.Block() as block:
            for ename in Builder.ENGINES:
                items = b.q[ename]

                def runner(eng, items=items):
                    waited = {}
                    for fn, waits, (semh, inc) in items:
                        for sname, v in waits:
                            if waited.get(sname, 0) < v:
                                eng.wait_ge(b.sems[sname], v)
                                waited[sname] = v
                        ins = fn(eng)
                        if semh is not None:
                            ins.then_inc(semh, inc)
                getattr(block, ename)(runner)
        ctx.close()
        return nc

    if DEBUG_TAP == "wdt_disabled":
        for grp in range(3):
            for i in range(8):
                dst = y_out[grp * 1024 + i * 128:grp * 1024 + (i + 1) * 128, :].rearrange(
                    "p (j f) -> p j f", f=128)
                b.dma("gpsimd", lambda g, dst=dst, grp=grp, i=i: g.dma_start(
                    out=dst, in_=wdT_blk[i, grp * 8:(grp + 1) * 8].rearrange("j p f -> p j f")),
                    reads=["wdT_blk"], sem_key="y", accum=True)
        total_y = b.cnt["d_y"]
        b.q["gpsimd"].append((lambda g: g.wait_ge(b.sems["d_y"], total_y), [], (None, 0)))
        with nc.Block() as block:
            for ename in Builder.ENGINES:
                items = b.q[ename]

                def runner(eng, items=items):
                    waited = {}
                    for fn, waits, (semh, inc) in items:
                        for sname, v in waits:
                            if waited.get(sname, 0) < v:
                                eng.wait_ge(b.sems[sname], v)
                                waited[sname] = v
                        ins = fn(eng)
                        if semh is not None:
                            ins.then_inc(semh, inc)
                getattr(block, ename)(runner)
        ctx.close()
        return nc

    b.alias_release("hh", ["big_sc_init"])
    b.alias_release("gfm", ["big_sc_init"])
    b.alias_release("ofin", ["big_sc_init"])
    b.alias_release("ofm", ["big_sc_init"])

    for s in range(NSLAB):
        xa, xares = (xa_sb, "xa") if s % 2 == 0 else (xb_sb, "xb")
        xb, xbres = (xb_sb, "xb") if s % 2 == 0 else (xa_sb, "xa")
        if s + 1 < NSLAB and DEBUG_TAP != "g0":
            b.dma("sync", x_load(xb, s + 1), writes=[xbres])

        if DEBUG_TAP == "x":
            for c in range(NCH):
                b.op("vector", lambda v, xa=xa, c=c: v.tensor_copy(
                    x1_sb[:, c * DM:(c + 1) * DM], xa[:, c * DM:(c + 1) * DM]),
                    reads=[xares], writes=["x1"])
            ydst = y_out[s * SLAB:(s + 1) * SLAB].rearrange("(c p) f -> p c f", p=128)
            b.dma("gpsimd", lambda g, ydst=ydst: g.dma_start(out=ydst, in_=x1_sb[:]),
                 reads=["x1"], sem_key="y", accum=True)
            continue
        if DEBUG_TAP in ("ln_disabled",):
            emit_ln(xa, xares, f"ln1_{s}")
            if DEBUG_TAP == "ln2":
                b.op("vector", lambda v: v.tensor_copy(x1_sb[:, 0:32], stats_sb[:]),
                     reads=["st_sums", "st_mean", "st_msq", "st_var", "st_lnv",
                            "st_rstd", "st_nb"], writes=["x1"])
                for c in range(1, NCH):
                    b.op("scalar", lambda sc, xa=xa, c=c: sc.activation(
                        x1_sb[:, c * DM:(c + 1) * DM], xa[:, c * DM:(c + 1) * DM],
                        AF.Identity, bias=NB[:, c:c + 1], scale=RSTD[:, c:c + 1]),
                        reads=[xares, "st_rstd", "st_nb"], writes=["x1"])
            elif DEBUG_TAP == "st":
                b.op("vector", lambda v: v.tensor_copy(x1_sb[:, 0:32], stats_sb[:]),
                     reads=["st_sums", "st_mean", "st_msq", "st_var", "st_lnv",
                            "st_rstd", "st_nb"], writes=["x1"])
                b.op("vector", lambda v: v.tensor_copy(x1_sb[:, 32:36], consts_sb[:]),
                     reads=["consts"], writes=["x1"])
            elif DEBUG_TAP == "ln":
                for c in range(NCH):
                    b.op("scalar", lambda sc, xa=xa, c=c: sc.activation(
                        x1_sb[:, c * DM:(c + 1) * DM], xa[:, c * DM:(c + 1) * DM],
                        AF.Identity, bias=NB[:, c:c + 1], scale=RSTD[:, c:c + 1]),
                        reads=[xares, "st_rstd", "st_nb"], writes=["x1"])
            else:
                for c in range(NCH):
                    def ytr(t, c=c):
                        for i in range(8):
                            ins = t.transpose(pT[:, i * 128:(i + 1) * 128],
                                              fm_sc[:, i * 512 + c * 128:i * 512 + (c + 1) * 128],
                                              ident_sb[:])
                        return ins
                    b.op("tensor", ytr, reads=["fm", "ident"], writes=["pT"])
                    b.op("vector", lambda v, c=c: v.tensor_copy(
                        x1_sb[:, c * DM:(c + 1) * DM], pT[:]),
                        reads=["pT"], writes=["x1"])
            ydst = y_out[s * SLAB:(s + 1) * SLAB].rearrange("(c p) f -> p c f", p=128)
            b.dma("gpsimd", lambda g, ydst=ydst: g.dma_start(out=ydst, in_=x1_sb[:]),
                 reads=["x1"], sem_key="y", accum=True)
            continue

        if s == 0:
            emit_ln_stats(xa, xares, 1)
        emit_ln_apply(xa, xares, 1)
        emit_ffn(xa, xares, x1_sb, "x1", 0)
        if DEBUG_TAP == "g0":
            ydst2 = y_out[0:512].rearrange("(c p) f -> p c f", p=128)
            b.dma("gpsimd", lambda g, ydst2=ydst2: g.dma_start(out=ydst2, in_=xb_sb[:]),
                  reads=["xb"], sem_key="y", accum=True)
            break
        if DEBUG_TAP == "hh":
            for m in range(24):
                b.dma("gpsimd", lambda g, m=m: g.dma_start(
                    out=y_out[m * 128:(m + 1) * 128, 0:512], in_=HH(m)),
                    reads=["hh"], sem_key="y", accum=True)
            ydst2 = y_out[3584:4096].rearrange("(c p) f -> p c f", p=128)
            b.dma("gpsimd", lambda g, ydst2=ydst2: g.dma_start(out=ydst2, in_=x1_sb[:]),
                  reads=["x1"], sem_key="y", accum=True)
            break
        if N_SUBLAYERS >= 2:
            b.alias_release("gfm", ["hh"])
            b.alias_release("ofin", ["hh"])
            b.alias_release("ofm", ["hh"])
            emit_ln_stats(x1_sb, "x1", 0, have_sums=True)
            emit_ln_apply(x1_sb, "x1", 0)
            emit_attn(x1_sb, "x1", xa, xares, s)
            if DEBUG_TAP == "of":
                ydst3 = y_out[0:512].rearrange("(c p) f -> p c f", p=128)
                b.dma("gpsimd", lambda g, ydst3=ydst3: g.dma_start(
                    out=ydst3, in_=OFIN.bitcast(F32) if False else OFIN),
                    reads=["ofin"], sem_key="y", accum=True)
                ydst4 = y_out[512:1024].rearrange("(c p) f -> p c f", p=128)
                b.dma("gpsimd", lambda g, ydst4=ydst4: g.dma_start(out=ydst4, in_=GFM),
                      reads=["gfm"], sem_key="y", accum=True)
                ydst5 = y_out[1024:1536].rearrange("(c p) f -> p c f", p=128)
                b.dma("gpsimd", lambda g, ydst5=ydst5: g.dma_start(out=ydst5, in_=qfm_sb[:]),
                      reads=["qfm"], sem_key="y", accum=True)
                ydst6 = y_out[1536:2048].rearrange("(c p) f -> p c f", p=128)
                b.dma("gpsimd", lambda g, ydst6=ydst6: g.dma_start(out=ydst6, in_=kfm_sb[:]),
                      reads=["kfm"], sem_key="y", accum=True)
                break
        if N_SUBLAYERS >= 3:
            b.alias_release("hh", ["gfm", "ofin", "ofm"])
            emit_ln_stats(xa, xares, 0, have_sums=True)
            emit_ln_apply(xa, xares, 0)
            if s + 1 < NSLAB:
                xnb, xnres = (xb_sb, "xb") if s % 2 == 0 else (xa_sb, "xa")
                emit_ln_stats(xnb, xnres, 1)
            emit_ffn(xa, xares, x1_sb, "x1", 1)
        outres = "x1" if N_SUBLAYERS != 2 else xares
        outbuf = x1_sb if N_SUBLAYERS != 2 else xa
        ydst = y_out[s * SLAB:(s + 1) * SLAB].rearrange("(c p) f -> p c f", p=128)
        b.dma("gpsimd", lambda g, outbuf=outbuf, ydst=ydst: g.dma_start(out=ydst, in_=outbuf[:]),
             reads=[outres], sem_key="y", accum=True)

    total_y = b.cnt["d_y"]
    b.q["gpsimd"].append((lambda g: g.wait_ge(b.sems["d_y"], total_y), [], (None, 0)))

    with nc.Block() as block:
        for ename in Builder.ENGINES:
            items = b.q[ename]

            def runner(eng, items=items):
                waited = {}
                for fn, waits, (semh, inc) in items:
                    for sname, v in waits:
                        if waited.get(sname, 0) < v:
                            eng.wait_ge(b.sems[sname], v)
                            waited[sname] = v
                    ins = fn(eng)
                    if semh is not None:
                        ins.then_inc(semh, inc)
            getattr(block, ename)(runner)

    ctx.close()
    return nc


_NC_CACHE = {}


def _get_nc():
    if "nc" not in _NC_CACHE:
        _NC_CACHE["nc"] = build_program()
    return _NC_CACHE["nc"]


def kernel(L0_attn_x, L0_attn_w_qkv, L0_attn_w_gate, L0_attn_w_out,
           L0_ffn_w_gate, L0_ffn_w_up, L0_ffn_w_down):
    x = np.ascontiguousarray(np.asarray(L0_attn_x, dtype=np.float32))
    w_qkv = np.ascontiguousarray(np.asarray(L0_attn_w_qkv, dtype=np.float32))
    w_gate = np.ascontiguousarray(np.asarray(L0_attn_w_gate, dtype=np.float32))
    w_out = np.ascontiguousarray(np.asarray(L0_attn_w_out, dtype=np.float32))
    wg = np.ascontiguousarray(np.asarray(L0_ffn_w_gate, dtype=np.int32))
    wu = np.ascontiguousarray(np.asarray(L0_ffn_w_up, dtype=np.int32))
    wd = np.ascontiguousarray(np.asarray(L0_ffn_w_down, dtype=np.int32))

    ident = np.eye(128, dtype=ml_dtypes.bfloat16)
    mask4 = np.zeros((128, 512), dtype=np.float32)
    tri = (np.arange(128)[:, None] <= np.arange(128)[None, :]).astype(np.float32)
    for j in range(4):
        mask4[:, j * 128:(j + 1) * 128] = tri
    consts = np.tile(np.array([[EPS_LN, -0.5, SFF, 1.0]], dtype=np.float32), (128, 1))
    zeros = np.zeros((128, 512), dtype=ml_dtypes.bfloat16)

    nc = _get_nc()
    in_maps = []
    for c in range(8):
        shard = np.ascontiguousarray(x[2 * c:2 * c + 2].reshape(TCORE, DM))
        in_maps.append({
            "x": shard, "w_qkv": w_qkv, "w_gate": w_gate, "w_out": w_out,
            "wg_int": wg, "wu_int": wu, "wd_int": wd,
            "ident": ident, "mask4": mask4, "consts": consts, "zeros": zeros,
        })
    res = run_bass_kernel_spmd(nc, in_maps, core_ids=list(range(8)))
    out = np.empty((16, 2048, DM), dtype=np.float32)
    for c in range(8):
        out[2 * c:2 * c + 2] = np.asarray(res.results[c]["y"]).reshape(2, 2048, DM)
    return out


if __name__ == "__main__":
    import reference as _ref
    inputs = _ref.setup_inputs()
    out = kernel(**{k: np.asarray(v) for k, v in inputs.items()})
    exp = np.asarray(_ref.reference(**inputs))
    err = np.linalg.norm(out - exp) / np.linalg.norm(exp)
    print("rel_err:", err)


# version 17
# speedup vs baseline: 1.0394x; 1.0394x over previous
import numpy as np
import ml_dtypes
import concourse.bass as bass
import concourse.mybir as mybir
from concourse.bass_utils import run_bass_kernel_spmd
from contextlib import ExitStack

F32 = mybir.dt.float32
BF16 = mybir.dt.bfloat16
I32 = mybir.dt.int32
AF = mybir.ActivationFunctionType
ALU = mybir.AluOpType
AX = mybir.AxisListType

DM = 1024
HID = 3072
NH = 16
HD = 64
TCORE = 4096
SLAB = 512
NSLAB = TCORE // SLAB
NCH = SLAB // 128
SEQ_SLABS = 2048 // SLAB
SFF = 1.0 / 127.0
C_DOWN = 0.5 * SFF * SFF
EPS_LN = 1e-5
EPS_DEN = 1e-6

N_SUBLAYERS = 3
DEBUG_TAP = None
ATTN_STAGE = 6


class Builder:

    ENGINES = ("sync", "gpsimd", "tensor", "vector", "scalar")

    def __init__(self, nc, ctx):
        self.nc = nc
        self.ctx = ctx
        self.q = {e: [] for e in self.ENGINES}
        self.sems = {}
        self.cnt = {}
        self.owner = {}
        for name, eng in [("sGPc", "gpsimd"), ("sT", "tensor"),
                          ("sV", "vector"), ("sA", "scalar")]:
            self._new_sem(name, eng)
        self.res = {}

    def _new_sem(self, name, eng):
        self.sems[name] = self.ctx.enter_context(self.nc.semaphore(name))
        self.cnt[name] = 0
        self.owner[name] = eng

    def _r(self, name):
        if name not in self.res:
            self.res[name] = {"ready": {}, "free": {}}
        return self.res[name]

    @staticmethod
    def _merge(dst, src):
        for s, v in src.items():
            if dst.get(s, 0) < v:
                dst[s] = v

    def alias_release(self, dst, srcs):
        d = self._r(dst)
        for s in srcs:
            r = self._r(s)
            self._merge(d["free"], r["free"])
            self._merge(d["free"], r["ready"])

    def _queue(self, eng, fn, reads, writes, sem, inc, extra, accum):
        waits = {}
        for r in reads:
            self._merge(waits, self._r(r)["ready"])
        for r in writes:
            self._merge(waits, self._r(r)["ready"])
            self._merge(waits, self._r(r)["free"])
        for s, v in extra:
            if waits.get(s, 0) < v:
                waits[s] = v
        waits = {s: v for s, v in waits.items()
                 if not (self.owner[s] == eng == "tensor")}
        for s, v in waits.items():
            assert v <= self.cnt[s], f"forward wait {s}={v} > {self.cnt[s]}"
        self.cnt[sem] += inc
        after = self.cnt[sem]
        self.q[eng].append((fn, sorted(waits.items()), (self.sems[sem], inc)))
        for r in reads:
            self._merge(self._r(r)["free"], {sem: after})
        for r in writes:
            if accum:
                self._merge(self._r(r)["ready"], {sem: after})
                self._merge(self._r(r)["free"], {sem: after})
            else:
                self.res[r]["ready"] = {sem: after}
                self.res[r]["free"] = {sem: after}
        return (sem, after)

    def op(self, eng, fn, reads=(), writes=(), sem=None, extra=()):
        if sem is None:
            sem = {"gpsimd": "sGPc", "tensor": "sT",
                   "vector": "sV", "scalar": "sA"}[eng]
        return self._queue(eng, fn, reads, writes, sem, 1, extra, False)

    def dma(self, eng, fn, reads=(), writes=(), sem_key=None, extra=(), accum=False):
        key = sem_key if sem_key is not None else writes[0]
        sem = "d_" + key
        if sem not in self.sems:
            self._new_sem(sem, None)
        return self._queue(eng, fn, reads, writes, sem, 16, extra, accum)

    def mark(self, *sems):
        return [(s, self.cnt[s]) for s in sems]


def build_program():
    nc = bass.Bass("TRN2", target_bir_lowering=False)

    x_in = nc.dram_tensor("x", [TCORE, DM], F32, kind="ExternalInput").ap()
    w_qkv = nc.dram_tensor("w_qkv", [DM, 3 * DM], F32, kind="ExternalInput").ap()
    w_gate = nc.dram_tensor("w_gate", [DM, DM], F32, kind="ExternalInput").ap()
    w_out = nc.dram_tensor("w_out", [DM, DM], F32, kind="ExternalInput").ap()
    wg_int = nc.dram_tensor("wg_int", [HID, DM], I32, kind="ExternalInput").ap()
    wu_int = nc.dram_tensor("wu_int", [HID, DM], I32, kind="ExternalInput").ap()
    wd_int = nc.dram_tensor("wd_int", [DM, HID], I32, kind="ExternalInput").ap()
    ident_in = nc.dram_tensor("ident", [128, 128], BF16, kind="ExternalInput").ap()
    mask_in = nc.dram_tensor("mask4", [128, 512], F32, kind="ExternalInput").ap()
    consts_in = nc.dram_tensor("consts", [128, 4], F32, kind="ExternalInput").ap()
    zeros_in = nc.dram_tensor("zeros", [128, 512], BF16, kind="ExternalInput").ap()
    y_out = nc.dram_tensor("y", [TCORE, DM], F32, kind="ExternalOutput").ap()

    wg_bf = nc.dram_tensor("wg_bf", [HID, DM], BF16).ap()
    wu_bf = nc.dram_tensor("wu_bf", [HID, DM], BF16).ap()
    wd_bf = nc.dram_tensor("wd_bf", [DM, HID], BF16).ap()
    wgT_blk = nc.dram_tensor("wgT_blk", [24, 128, 8, 128], BF16).ap()
    wuT_blk = nc.dram_tensor("wuT_blk", [24, 128, 8, 128], BF16).ap()
    wdT_blk = nc.dram_tensor("wdT_blk", [8, 128, 24, 128], BF16).ap()
    wgate_blk = nc.dram_tensor("wgate_blk", [8, 128, 8, 128], BF16).ap()
    wout_blk = nc.dram_tensor("wout_blk", [8, 128, 8, 128], BF16).ap()

    ctx = ExitStack()
    e = ctx.enter_context

    wqkv_sb = e(nc.sbuf_tensor("wqkv_sb", [128, 8 * 3072], BF16))
    xa_sb = e(nc.sbuf_tensor("xa_sb", [128, NCH * DM], F32))
    xb_sb = e(nc.sbuf_tensor("xb_sb", [128, NCH * DM], F32))
    x1_sb = e(nc.sbuf_tensor("x1_sb", [128, NCH * DM], F32))
    fm_sc = e(nc.sbuf_tensor("fm_sc", [128, 8 * SLAB], BF16))
    big_sc = e(nc.sbuf_tensor("big_sc", [128, 24 * SLAB], BF16))
    wg_buf = e(nc.sbuf_tensor("wg_buf", [128, 3 * 1024], BF16))
    wu_buf = e(nc.sbuf_tensor("wu_buf", [128, 3 * 1024], BF16))
    wd_buf = e(nc.sbuf_tensor("wd_buf", [128, 2 * 3072], BF16))
    wga_buf = e(nc.sbuf_tensor("wga_buf", [128, 2 * 1024], BF16))
    wou_buf = e(nc.sbuf_tensor("wou_buf", [128, 2 * 1024], BF16))
    qfm_sb = e(nc.sbuf_tensor("qfm_sb", [128, 8 * SLAB], BF16))
    kfm_sb = e(nc.sbuf_tensor("kfm_sb", [128, 8 * SLAB], BF16))
    ktm_sb = e(nc.sbuf_tensor("ktm_sb", [128, 2 * DM], BF16))
    vaug_sb = e(nc.sbuf_tensor("vaug_sb", [128, 2 * NH * (HD + 1)], BF16))
    gtm_sb = e(nc.sbuf_tensor("gtm_sb", [128, 2 * DM], BF16))
    state_sb = e(nc.sbuf_tensor("state_sb", [128, 8 * 128], BF16))
    st_sb = e(nc.sbuf_tensor("st_sb", [128, 2 * SLAB], BF16))
    ytm_sb = e(nc.sbuf_tensor("ytm_sb", [128, 2 * DM], BF16))
    sg_sb = e(nc.sbuf_tensor("sg_sb", [128, 2 * SLAB], BF16))
    r_sb = e(nc.sbuf_tensor("r_sb", [128, SLAB], F32))
    d_sb = e(nc.sbuf_tensor("d_sb", [128, SLAB], F32))
    e_sb = e(nc.sbuf_tensor("e_sb", [128, SLAB], F32))
    stats_sb = e(nc.sbuf_tensor("stats_sb", [128, 64], F32))
    den_sb = e(nc.sbuf_tensor("den_sb", [128, 8], F32))
    consts_sb = e(nc.sbuf_tensor("consts_sb", [128, 4], F32))
    mask_sb = e(nc.sbuf_tensor("mask_sb", [128, 512], F32))
    ident_sb = e(nc.sbuf_tensor("ident_sb", [128, 128], BF16))
    zeros_sb = e(nc.sbuf_tensor("zeros_sb", [128, 512], BF16))

    pS = e(nc.psum_tensor("pS", [128, 1024], F32))
    pA = e(nc.psum_tensor("pA", [128, 512], F32))
    pB = e(nc.psum_tensor("pB", [128, 512], F32))
    pC = e(nc.psum_tensor("pC", [128, 512], F32))
    pT = e(nc.psum_tensor("pT", [128, 1024], BF16))
    pT2 = e(nc.psum_tensor("pT2", [128, 1024], BF16))
    pO = e(nc.psum_tensor("pO", [128, 512], F32))

    import sys as _sys
    print("SBUF remaining after allocs:", nc.sbuf_bytes_remaining, file=_sys.stderr)
    b = Builder(nc, ctx)

    BANK3 = ((pA, "pA"), (pB, "pB"), (pO, "pO"))

    _ptst = {"i": 0}

    def pt_next():
        _ptst["i"] ^= 1
        return (pT, "pT") if _ptst["i"] else (pT2, "pT2")

    HH = lambda m: big_sc[:, m * 512:(m + 1) * 512]
    OFM = big_sc[:, 0:4096]
    OFIN = big_sc[:, 4096:8192]
    GFM = big_sc[:, 8192:12288]

    def _stats(par):
        o = par * 32
        return dict(SUMS=stats_sb[:, o:o + 4], SUMSQ=stats_sb[:, o + 4:o + 8],
                    MEAN=stats_sb[:, o + 8:o + 12], MSQ=stats_sb[:, o + 12:o + 16],
                    VAR=stats_sb[:, o + 16:o + 20], LNV=stats_sb[:, o + 20:o + 24],
                    RSTD=stats_sb[:, o + 24:o + 28], NB=stats_sb[:, o + 28:o + 32])
    EPSC = consts_sb[:, 0:1]
    NEGH = consts_sb[:, 1:2]
    SFFC = consts_sb[:, 2:3]

    b.dma("sync", lambda s: s.dma_start(out=consts_sb[:], in_=consts_in), writes=["consts"])
    b.dma("sync", lambda s: s.dma_start(out=zeros_sb[:], in_=zeros_in), writes=["zeros"])
    b.op("gpsimd", lambda g: g.memset(vaug_sb[:], 1.0), writes=["vaug0", "vaug1"])
    b.dma("sync", lambda s: s.dma_start(out=ident_sb[:], in_=ident_in), writes=["ident"])
    b.dma("sync", lambda s: s.dma_start(out=mask_sb[:], in_=mask_in), writes=["mask"])

    def x_load(buf, s):
        src = x_in[s * SLAB:(s + 1) * SLAB].rearrange("(c p) f -> p c f", p=128)
        return lambda g: g.dma_start(out=buf[:], in_=src)
    b.dma("sync", x_load(xa_sb, 0), writes=["xa"])

    b.dma("gpsimd", lambda g: g.dma_start(out=wg_bf, in_=wg_int), writes=["wg_bf"])
    b.dma("gpsimd", lambda g: g.dma_start(out=wu_bf, in_=wu_int), writes=["wu_bf"])
    b.dma("gpsimd", lambda g: g.dma_start(out=wd_bf, in_=wd_int), writes=["wd_bf"])
    wga_src = w_gate.rearrange("(kd p) g -> p kd g", p=128)
    wou_src = w_out.rearrange("(kc p) g -> p kc g", p=128)
    for gc in range(8):
        b.dma("gpsimd", lambda g, gc=gc: g.dma_start(
            out=wgate_blk[gc], in_=wga_src[:, :, gc * 128:(gc + 1) * 128]),
            writes=["wgate_blk"])
    for m in range(8):
        b.dma("gpsimd", lambda g, m=m: g.dma_start(
            out=wout_blk[m], in_=wou_src[:, :, m * 128:(m + 1) * 128]),
            writes=["wout_blk"])
    for kd in range(8):
        b.dma("gpsimd", lambda g, kd=kd: g.dma_start(
            out=wqkv_sb[:, kd * 3072:(kd + 1) * 3072],
            in_=w_qkv[kd * 128:(kd + 1) * 128, :]), writes=["wqkv"])

    for wsrc, wdst, wres, dres in ((wg_bf, wgT_blk, "wg_bf", "wgT_blk"),
                                   (wu_bf, wuT_blk, "wu_bf", "wuT_blk")):
        for k in range(8):
            slot = (k % 2) * 3072
            buf = big_sc[:, slot:slot + 3072]
            b.dma("sync", lambda s, wsrc=wsrc, k=k, buf=buf: s.dma_start(
                out=buf, in_=wsrc[:, k * 128:(k + 1) * 128], transpose=True),
                reads=[wres], writes=[f"ph0_{k % 2}"])
            b.dma("scalar", lambda s, wdst=wdst, k=k, buf=buf: s.dma_start(
                out=wdst[:, :, k].rearrange("m p f -> p m f"), in_=buf),
                reads=[f"ph0_{k % 2}"], writes=[dres], sem_key=f"{dres}{k % 2}", accum=True)
    for j in range(24):
        slot = 6144 + (j % 2) * 1024
        buf = big_sc[:, slot:slot + 1024]
        b.dma("sync", lambda s, j=j, buf=buf: s.dma_start(
            out=buf, in_=wd_bf[:, j * 128:(j + 1) * 128], transpose=True),
            reads=["wd_bf"], writes=[f"ph0d_{j % 2}"])
        b.dma("scalar", lambda s, j=j, buf=buf: s.dma_start(
            out=wdT_blk[:, :, j].rearrange("i p f -> p i f"), in_=buf),
            reads=[f"ph0d_{j % 2}"], writes=["wdT_blk"], sem_key=f"wdT_blk{j % 2}", accum=True)
    b.alias_release("big_sc_init", ["ph0_0", "ph0_1", "ph0d_0", "ph0d_1"])

    pS_guard = {"last_update": None, "last_evac": None}

    def emit_ln_stats(xbuf, xres, par, have_sums=False, have_sumsq=False):
        S = _stats(par)
        p = f"p{par}"
        for c in range(NCH):
            if have_sumsq:
                break
            xc = xbuf[:, c * DM:(c + 1) * DM]
            b.op("scalar", lambda s, xc=xc, c=c, S=S: s.activation(
                ktm_sb[:, 0:DM], xc, AF.Square, accum_out=S["SUMSQ"][:, c:c + 1]),
                reads=[xres], writes=["sq", "st_sums" + p])
            if not have_sums:
                b.op("vector", lambda v, xc=xc, c=c, S=S: v.reduce_sum(
                    S["SUMS"][:, c:c + 1], xc, axis=AX.X),
                    reads=[xres], writes=["st_sums" + p])
        b.op("vector", lambda v, S=S: v.tensor_scalar(S["MEAN"], S["SUMS"], 1.0 / DM, None, ALU.mult),
             reads=["st_sums" + p], writes=["st_mean" + p])
        b.op("vector", lambda v, S=S: v.tensor_mul(S["MSQ"], S["MEAN"], S["MEAN"]),
             reads=["st_mean" + p], writes=["st_msq" + p])
        b.op("vector", lambda v, S=S: v.scalar_tensor_tensor(
            S["VAR"], S["SUMSQ"], 1.0 / DM, S["MSQ"], ALU.mult, ALU.subtract),
            reads=["st_sums" + p, "st_msq" + p], writes=["st_var" + p])
        b.op("scalar", lambda s, S=S: s.activation(S["LNV"], S["VAR"], AF.Ln, bias=EPSC, scale=1.0),
             reads=["st_var" + p, "consts"], writes=["st_lnv" + p])
        b.op("scalar", lambda s, S=S: s.activation(S["RSTD"], S["LNV"], AF.Exp, bias=0.0, scale=NEGH),
             reads=["st_lnv" + p, "consts"], writes=["st_rstd" + p])
        b.op("vector", lambda v, S=S: v.scalar_tensor_tensor(
            S["NB"], S["MEAN"], -1.0, S["RSTD"], ALU.mult, ALU.mult),
            reads=["st_mean" + p, "st_rstd" + p], writes=["st_nb" + p])

    def emit_vsq(xbuf, xres, c):
        S = _stats(0)
        b.op("vector", lambda v, c=c, S=S: v.scalar_tensor_tensor(
            ktm_sb[:, DM:2 * DM], xbuf[:, c * DM:(c + 1) * DM], 1.0,
            xbuf[:, c * DM:(c + 1) * DM], ALU.mult, ALU.mult,
            accum_out=S["SUMSQ"][:, c:c + 1]),
            reads=[xres], writes=["sq", "st_sumsp0"])

    def emit_ln_apply(xbuf, xres, par):
        S = _stats(par)
        p = f"p{par}"
        for c in range(NCH):
            xc = xbuf[:, c * DM:(c + 1) * DM]
            yslot = ytm_sb[:, (c % 2) * DM:((c % 2) + 1) * DM]
            b.op("scalar", lambda s, xc=xc, c=c, yslot=yslot, S=S: s.activation(
                yslot, xc, AF.Identity, bias=S["NB"][:, c:c + 1], scale=S["RSTD"][:, c:c + 1]),
                reads=[xres, "st_rstd" + p, "st_nb" + p], writes=[f"ytm{c % 2}"])

            ptile, pres = pt_next()

            def transp(t, c=c, yslot=yslot, ptile=ptile):
                for i in range(8):
                    ins = t.transpose(ptile[:, i * 128:(i + 1) * 128],
                                      yslot[:, i * 128:(i + 1) * 128], ident_sb[:])
                return ins
            b.op("tensor", transp, reads=[f"ytm{c % 2}", "ident"], writes=[pres])
            dst = fm_sc.ap().rearrange("p (k cf) -> p k cf", k=8)[:, :, c * 128:(c + 1) * 128]
            src = ptile.ap().rearrange("p (k f) -> p k f", k=8)
            b.op("scalar", lambda s, dst=dst, src=src: s.copy(dst, src),
                 reads=[pres], writes=["fm"])

    def emit_ffn(xbuf, xres, outbuf, outres, sub):
        def wg_dma(m):
            b.dma("sync", lambda s, m=m: s.dma_start(
                out=wg_buf[:, (m % 3) * 1024:((m % 3) + 1) * 1024],
                in_=wgT_blk[m]),
                reads=["wgT_blk"], writes=[f"wg{m % 3}"])
            b.dma("sync", lambda s, m=m: s.dma_start(
                out=wu_buf[:, (m % 3) * 1024:((m % 3) + 1) * 1024],
                in_=wuT_blk[m]),
                reads=["wuT_blk"], writes=[f"wu{m % 3}"])
        wg_dma(0)
        wg_dma(1)
        for m in range(24):
            if m + 2 < 24:
                wg_dma(m + 2)
            if m < 2:
                b.dma("sync", lambda s, m=m: s.dma_start(
                    out=wd_buf[:, (m % 2) * 3072:((m % 2) + 1) * 3072],
                    in_=wdT_blk[m]),
                    reads=["wdT_blk"], writes=[f"wd{m % 2}"])
            bank, bres = BANK3[m % 3]
            wslot = wg_buf[:, (m % 3) * 1024:((m % 3) + 1) * 1024]

            def gmm(t, bank=bank, wslot=wslot):
                for k in range(8):
                    ins = t.matmul(bank[:], wslot[:, k * 128:(k + 1) * 128],
                                   fm_sc[:, k * 512:(k + 1) * 512],
                                   start=(k == 0), stop=(k == 7))
                return ins
            b.op("tensor", gmm, reads=[f"wg{m % 3}", "fm"], writes=[bres])
            if DEBUG_TAP == "g0" and m == 0 and sub == 0:
                b.op("scalar", lambda sc, bank=bank: sc.copy(xb_sb[:, 0:512], bank[:]),
                     reads=[bres], writes=["xb"])
                b.op("vector", lambda v, wslot=wslot: v.tensor_copy(
                    xb_sb[:, 1024:2048], wslot), reads=[f"wg{m % 2}"], writes=["xb"])
                b.op("vector", lambda v: v.tensor_copy(
                    xb_sb[:, 2048:2560], fm_sc[:, 0:512]),
                    reads=["fm"], writes=["xb"])
            uslot = wu_buf[:, (m % 3) * 1024:((m % 3) + 1) * 1024]

            def umm(t, uslot=uslot):
                for k in range(8):
                    ins = t.matmul(pC[:], uslot[:, k * 128:(k + 1) * 128],
                                   fm_sc[:, k * 512:(k + 1) * 512],
                                   start=(k == 0), stop=(k == 7))
                return ins
            b.op("tensor", umm, reads=[f"wu{m % 3}", "fm"], writes=["pC"])
            sgslot = sg_sb[:, (m % 2) * 512:((m % 2) + 1) * 512]
            b.op("scalar", lambda s, bank=bank, sgslot=sgslot: s.activation(
                sgslot, bank[:], AF.Silu, scale=SFFC),
                reads=[bres, "consts"], writes=[f"sg{m % 2}"])
            b.op("vector", lambda v, m=m, sgslot=sgslot: v.tensor_tensor(
                HH(m), sgslot, pC[:], ALU.mult),
                reads=[f"sg{m % 2}", "pC"], writes=["hh"])
        for i in range(8):
            bank, bres = BANK3[i % 3]
            wslot = wd_buf[:, (i % 2) * 3072:((i % 2) + 1) * 3072]

            def dmm(t, bank=bank, wslot=wslot):
                for j in range(24):
                    ins = t.matmul(bank[:], wslot[:, j * 128:(j + 1) * 128], HH(j),
                                   start=(j == 0), stop=(j == 23))
                return ins
            b.op("tensor", dmm, reads=[f"wd{i % 2}", "hh"], writes=[bres])
            if i + 2 < 8:
                b.dma("sync", lambda s, i=i: s.dma_start(
                    out=wd_buf[:, (i % 2) * 3072:((i % 2) + 1) * 3072],
                    in_=wdT_blk[i + 2]),
                    reads=["wdT_blk"], writes=[f"wd{i % 2}"])
            b.op("scalar", lambda s, bank=bank, i=i: s.copy(
                fm_sc[:, i * 512:(i + 1) * 512], bank[:]),
                reads=[bres], writes=["fm"])
        for c in range(NCH):
            ptile, pres = pt_next()

            def dtr(t, c=c, ptile=ptile):
                for i in range(8):
                    ins = t.transpose(ptile[:, i * 128:(i + 1) * 128],
                                      fm_sc[:, i * 512 + c * 128:i * 512 + (c + 1) * 128],
                                      ident_sb[:])
                return ins
            b.op("tensor", dtr, reads=["fm", "ident"], writes=[pres])
            if sub == 0:
                b.op("vector", lambda v, c=c, ptile=ptile: v.scalar_tensor_tensor(
                    outbuf[:, c * DM:(c + 1) * DM], ptile[:], C_DOWN,
                    xbuf[:, c * DM:(c + 1) * DM], ALU.mult, ALU.add,
                    accum_out=_stats(0)["SUMS"][:, c:c + 1]),
                    reads=[pres, xres], writes=[outres, "st_sumsp0"])
                emit_vsq(outbuf, outres, c)
            else:
                b.op("vector", lambda v, c=c, ptile=ptile: v.scalar_tensor_tensor(
                    outbuf[:, c * DM:(c + 1) * DM], ptile[:], C_DOWN,
                    xbuf[:, c * DM:(c + 1) * DM], ALU.mult, ALU.add),
                    reads=[pres, xres], writes=[outres])

    def emit_attn(xbuf, xres, outbuf, outres, slab):
        seq_start = (slab % SEQ_SLABS == 0)
        def wga_dma(gc):
            b.dma("sync", lambda s, gc=gc: s.dma_start(
                out=wga_buf[:, (gc % 2) * 1024:((gc % 2) + 1) * 1024],
                in_=wgate_blk[gc]),
                reads=["wgate_blk"], writes=[f"wga{gc % 2}"])
        wga_dma(0)
        wga_dma(1)
        for which, dst, dres, off in (("q", qfm_sb, "qfm", 0), ("k", kfm_sb, "kfm", 1024)):
            for qc in range(8):
                bank, bres = BANK3[qc % 3]

                def pmm(t, bank=bank, qc=qc, off=off):
                    for k in range(8):
                        ins = t.matmul(bank[:],
                                       wqkv_sb[:, k * 3072 + off + qc * 128: k * 3072 + off + (qc + 1) * 128],
                                       fm_sc[:, k * 512:(k + 1) * 512],
                                       start=(k == 0), stop=(k == 7))
                    return ins
                b.op("tensor", pmm, reads=["wqkv", "fm"], writes=[bres])
                b.op("scalar", lambda s, bank=bank: s.activation(r_sb[:], bank[:], AF.Relu),
                     reads=[bres], writes=["r"])
                b.op("vector", lambda v, bank=bank: v.tensor_tensor(d_sb[:], bank[:], r_sb[:], ALU.subtract),
                     reads=[bres, "r"], writes=["d"])
                b.op("scalar", lambda s: s.activation(e_sb[:], d_sb[:], AF.Exp),
                     reads=["d"], writes=["eexp"])
                b.op("vector", lambda v, dst=dst, qc=qc: v.tensor_tensor(
                    dst[:, qc * 512:(qc + 1) * 512], e_sb[:], r_sb[:], ALU.add),
                    reads=["eexp", "r"], writes=[dres])
        if ATTN_STAGE < 1:
            return
        for gc in range(8):
            bank, bres = BANK3[gc % 3]

            def gamm(t, bank=bank, gc=gc):
                for k in range(8):
                    ins = t.matmul(bank[:],
                                   wga_buf[:, (gc % 2) * 1024 + k * 128:(gc % 2) * 1024 + (k + 1) * 128],
                                   fm_sc[:, k * 512:(k + 1) * 512],
                                   start=(k == 0), stop=(k == 7))
                return ins
            b.op("tensor", gamm, reads=[f"wga{gc % 2}", "fm"], writes=[bres])
            if gc + 2 < 8:
                wga_dma(gc + 2)
            b.op("scalar", lambda s, bank=bank, gc=gc: s.copy(
                GFM[:, gc * 512:(gc + 1) * 512], bank[:]),
                reads=[bres], writes=["gfm"])

        if ATTN_STAGE < 2:
            return

        def emit_prep(c):
            cpar = c % 2
            ptile, pres = pt_next()

            def ktr(t, c=c, ptile=ptile):
                for qc in range(8):
                    ins = t.transpose(ptile[:, qc * 128:(qc + 1) * 128],
                                      kfm_sb[:, qc * 512 + c * 128:qc * 512 + (c + 1) * 128],
                                      ident_sb[:])
                return ins
            b.op("tensor", ktr, reads=["kfm", "ident"], writes=[pres])
            b.op("scalar", lambda s, cpar=cpar, ptile=ptile: s.copy(
                ktm_sb[:, cpar * DM:(cpar + 1) * DM], ptile[:]),
                reads=[pres], writes=[f"ktm{cpar}"])

            def vtmm0(t, c=c):
                for k in range(8):
                    ins = t.matmul(pC[:],
                                   fm_sc[:, k * 512 + c * 128:k * 512 + (c + 1) * 128],
                                   wqkv_sb[:, k * 3072 + 2048:k * 3072 + 2048 + 512],
                                   start=(k == 0), stop=(k == 7))
                return ins
            b.op("tensor", vtmm0, reads=["wqkv", "fm"], writes=["pC"])
            vdst = vaug_sb.ap().rearrange("p (s h g) -> p s h g", s=2, h=NH)[
                :, cpar, 0:8, 0:HD]
            vsrc = pC.ap().rearrange("p (h g) -> p h g", h=8)
            b.op("scalar", lambda s, vdst=vdst, vsrc=vsrc: s.copy(vdst, vsrc),
                 reads=["pC"], writes=[f"vaug{cpar}"])
            ptile, pres = pt_next()

            def gtr(t, c=c, ptile=ptile):
                for i in range(8):
                    ins = t.transpose(ptile[:, i * 128:(i + 1) * 128],
                                      GFM[:, i * 512 + c * 128:i * 512 + (c + 1) * 128],
                                      ident_sb[:])
                return ins
            b.op("tensor", gtr, reads=["gfm", "ident"], writes=[pres])
            b.op("scalar", lambda s, cpar=cpar, ptile=ptile: s.activation(
                gtm_sb[:, cpar * DM:(cpar + 1) * DM], ptile[:], AF.Sigmoid),
                reads=[pres], writes=[f"gtm{cpar}"])

        def emit_vtm1(c):
            cpar = c % 2

            def vtmm1(t, c=c):
                for k in range(8):
                    ins = t.matmul(pO[:],
                                   fm_sc[:, k * 512 + c * 128:k * 512 + (c + 1) * 128],
                                   wqkv_sb[:, k * 3072 + 2560:k * 3072 + 2560 + 512],
                                   start=(k == 0), stop=(k == 7))
                return ins
            b.op("tensor", vtmm1, reads=["wqkv", "fm"], writes=["pO"])
            vdst = vaug_sb.ap().rearrange("p (s h g) -> p s h g", s=2, h=NH)[
                :, cpar, 8:16, 0:HD]
            vsrc = pO.ap().rearrange("p (h g) -> p h g", h=8)
            b.op("vector", lambda v, vdst=vdst, vsrc=vsrc: v.tensor_copy(vdst, vsrc),
                 reads=["pO"], writes=[f"vaug{cpar}"])

        HEAD_GROUPS = ([0, 2, 4, 6], [8, 10, 12, 14], [1, 3, 5, 7], [9, 11, 13, 15])
        emit_prep(0)
        emit_vtm1(0)
        for c in range(NCH):
            cpar = c % 2
            first = seq_start and c == 0
            if ATTN_STAGE < 3:
                continue
            if ATTN_STAGE >= 5 and not first:
                extra = [pS_guard["last_update"]] if pS_guard["last_update"] else []
                ev = b.op("scalar", lambda s: s.copy(state_sb[:], pS[:]),
                          writes=["state"], extra=extra)
                pS_guard["last_evac"] = ev

            def emit_st(g):
                gpar = g % 2
                heads = HEAD_GROUPS[g]
                bank, bres = (pA, "pA") if gpar == 0 else (pB, "pB")

                def stmm(t, c=c, bank=bank, heads=heads):
                    t.matmul(bank[:, 0:128], zeros_sb[:, 0:128], zeros_sb[:, 0:128],
                             start=True, stop=True)
                    for jj, h in enumerate(heads):
                        pb = (h % 2) * 64
                        qc = h // 2
                        ins = t.matmul(bank[:, jj * 128:(jj + 1) * 128],
                                       kfm_sb[pb:pb + 64, qc * 512 + c * 128:qc * 512 + (c + 1) * 128],
                                       qfm_sb[pb:pb + 64, qc * 512 + c * 128:qc * 512 + (c + 1) * 128],
                                       start=True, stop=True)
                    return ins
                b.op("tensor", stmm, reads=["kfm", "qfm", "zeros"], writes=[bres])
                b.op("vector", lambda v, bank=bank, gpar=gpar: v.tensor_tensor(
                    st_sb[:, gpar * 512:(gpar + 1) * 512], bank[:], mask_sb[:], ALU.mult),
                    reads=[bres, "mask"], writes=[f"st{gpar}"])

            def emit_o(g):
                gpar = g % 2
                heads = HEAD_GROUPS[g]
                omm_first = first or ATTN_STAGE < 5

                def omm(t, c=c, gpar=gpar, first=omm_first, heads=heads):
                    for jj, h in enumerate(heads):
                        pb = (h % 2) * 64
                        qc = h // 2
                        ins = t.matmul(pO[:, jj * 65:jj * 65 + 65],
                                       st_sb[:, gpar * 512 + jj * 128:gpar * 512 + (jj + 1) * 128],
                                       vaug_sb[:, (c % 2) * NH * 65 + h * 65:(c % 2) * NH * 65 + (h + 1) * 65],
                                       start=True, stop=first)
                        if not first:
                            ins = t.matmul(pO[:, jj * 65:jj * 65 + 65],
                                           qfm_sb[pb:pb + 64, qc * 512 + c * 128:qc * 512 + (c + 1) * 128],
                                           state_sb[pb:pb + 64, (h // 2) * 128:(h // 2) * 128 + 65],
                                           start=False, stop=True)
                    return ins
                o_reads = [f"st{gpar}", f"vaug{cpar}", "qfm"] + ([] if omm_first else ["state"])
                b.op("tensor", omm, reads=o_reads, writes=["pO"])
                dencol = pO[:, 0:260].rearrange("p (j g) -> p j g", g=65)[:, :, 64:65]
                b.op("vector", lambda v, dencol=dencol: v.tensor_scalar(
                    den_sb[:, 0:4].rearrange("p (j o) -> p j o", j=4), dencol,
                    EPS_DEN, None, ALU.add),
                    reads=["pO"], writes=["den"])
                b.op("vector", lambda v: v.reciprocal(den_sb[:, 4:8], den_sb[:, 0:4]),
                     reads=["den"], writes=["recip"])

                def epi(v, c=c, cpar=cpar, heads=heads):
                    for jj, h in enumerate(heads):
                        ins = v.scalar_tensor_tensor(
                            OFIN[:, c * DM + h * HD:c * DM + (h + 1) * HD],
                            pO[:, jj * 65:jj * 65 + 64], den_sb[:, 4 + jj:5 + jj],
                            gtm_sb[:, cpar * DM + h * HD:cpar * DM + (h + 1) * HD],
                            ALU.mult, ALU.mult)
                    return ins
                b.op("vector", epi, reads=["pO", "recip", f"gtm{cpar}"], writes=["ofin"])

            if ATTN_STAGE < 4:
                for g in range(4):
                    emit_st(g)
            else:
                emit_st(0)
                emit_st(1)
                emit_o(0)
                emit_st(2)
                emit_o(1)
                emit_st(3)
                if c + 1 < NCH:
                    emit_prep(c + 1)
                emit_o(2)
                emit_o(3)
                if c + 1 < NCH:
                    emit_vtm1(c + 1)
            if ATTN_STAGE < 5:
                continue
            if first:
                def zmm(t):
                    t.matmul(pS[:, 0:512], zeros_sb[:, 0:128], zeros_sb[:, 0:512],
                             start=True, stop=True)
                    return t.matmul(pS[:, 512:1024], zeros_sb[:, 0:128], zeros_sb[:, 0:512],
                                    start=True, stop=True)
                extra = [pS_guard["last_evac"]] if pS_guard["last_evac"] else []
                b.op("tensor", zmm, reads=["zeros"], extra=extra)

            def upd(t, c=c, cpar=cpar):
                for h in range(NH):
                    rb = (h % 2) * 64
                    tp = (0, 64) if rb == 64 else None
                    ins = t.matmul(pS[rb:rb + 64, (h // 2) * 128:(h // 2) * 128 + 65],
                                   ktm_sb[:, cpar * DM + h * HD:cpar * DM + (h + 1) * HD],
                                   vaug_sb[:, cpar * NH * 65 + h * 65:cpar * NH * 65 + (h + 1) * 65],
                                   start=False, stop=True, tile_position=tp)
                return ins
            extra = [pS_guard["last_evac"]] if pS_guard["last_evac"] else []
            up = b.op("tensor", upd, reads=[f"ktm{cpar}", f"vaug{cpar}"], extra=extra)
            pS_guard["last_update"] = up
            ptile, pres = pt_next()

            def otr(t, c=c, ptile=ptile):
                for i in range(8):
                    ins = t.transpose(ptile[:, i * 128:(i + 1) * 128],
                                      OFIN[:, c * DM + i * 128:c * DM + (i + 1) * 128],
                                      ident_sb[:])
                return ins
            b.op("tensor", otr, reads=["ofin", "ident"], writes=[pres])
            odst = OFM.rearrange("p (k cf) -> p k cf", k=8)[:, :, c * 128:(c + 1) * 128]
            osrc = ptile.ap().rearrange("p (k f) -> p k f", k=8)
            b.op("scalar", lambda s, odst=odst, osrc=osrc: s.copy(odst, osrc),
                 reads=[pres], writes=["ofm"])

        if ATTN_STAGE < 6:
            return
        def wou_dma(m):
            b.dma("sync", lambda s, m=m: s.dma_start(
                out=wou_buf[:, (m % 2) * 1024:((m % 2) + 1) * 1024],
                in_=wout_blk[m]),
                reads=["wout_blk"], writes=[f"wou{m % 2}"])
        wou_dma(0)
        wou_dma(1)
        for m in range(8):
            bank, bres = BANK3[m % 3]

            def opmm(t, bank=bank, m=m):
                for kc in range(8):
                    ins = t.matmul(bank[:],
                                   wou_buf[:, (m % 2) * 1024 + kc * 128:(m % 2) * 1024 + (kc + 1) * 128],
                                   OFM[:, kc * 512:(kc + 1) * 512],
                                   start=(kc == 0), stop=(kc == 7))
                return ins
            b.op("tensor", opmm, reads=[f"wou{m % 2}", "ofm"], writes=[bres])
            if m + 2 < 8:
                wou_dma(m + 2)
            b.op("scalar", lambda s, bank=bank, m=m: s.copy(
                fm_sc[:, m * 512:(m + 1) * 512], bank[:]),
                reads=[bres], writes=["fm"])
        for c in range(NCH):
            ptile, pres = pt_next()

            def atr(t, c=c, ptile=ptile):
                for i in range(8):
                    ins = t.transpose(ptile[:, i * 128:(i + 1) * 128],
                                      fm_sc[:, i * 512 + c * 128:i * 512 + (c + 1) * 128],
                                      ident_sb[:])
                return ins
            b.op("tensor", atr, reads=["fm", "ident"], writes=[pres])
            b.op("vector", lambda v, c=c, ptile=ptile: v.scalar_tensor_tensor(
                outbuf[:, c * DM:(c + 1) * DM], ptile[:], 1.0,
                xbuf[:, c * DM:(c + 1) * DM], ALU.mult, ALU.add,
                accum_out=_stats(0)["SUMS"][:, c:c + 1]),
                reads=[pres, xres], writes=[outres, "st_sumsp0"])
            emit_vsq(outbuf, outres, c)

    if DEBUG_TAP == "wgt_disabled":
        for m in range(24):
            dst = y_out[m * 128:(m + 1) * 128, :].rearrange("p (k f) -> p k f", f=128)
            b.dma("gpsimd", lambda g, dst=dst, m=m: g.dma_start(
                out=dst, in_=wgT_blk[m]),
                reads=["wgT_blk"], sem_key="y", accum=True)
        for m in range(8):
            dst = y_out[3072 + m * 128:3072 + (m + 1) * 128, :].rearrange("p (k f) -> p k f", f=128)
            b.dma("gpsimd", lambda g, dst=dst, m=m: g.dma_start(
                out=dst, in_=wgate_blk[m].rearrange("k p f -> p k f")),
                reads=["wgate_blk"], sem_key="y", accum=True)
        total_y = b.cnt["d_y"]
        b.q["gpsimd"].append((lambda g: g.wait_ge(b.sems["d_y"], total_y), [], (None, 0)))
        with nc.Block() as block:
            for ename in Builder.ENGINES:
                items = b.q[ename]

                def runner(eng, items=items):
                    waited = {}
                    for fn, waits, (semh, inc) in items:
                        for sname, v in waits:
                            if waited.get(sname, 0) < v:
                                eng.wait_ge(b.sems[sname], v)
                                waited[sname] = v
                        ins = fn(eng)
                        if semh is not None:
                            ins.then_inc(semh, inc)
                getattr(block, ename)(runner)
        ctx.close()
        return nc

    if DEBUG_TAP == "wdt_disabled":
        for grp in range(3):
            for i in range(8):
                dst = y_out[grp * 1024 + i * 128:grp * 1024 + (i + 1) * 128, :].rearrange(
                    "p (j f) -> p j f", f=128)
                b.dma("gpsimd", lambda g, dst=dst, grp=grp, i=i: g.dma_start(
                    out=dst, in_=wdT_blk[i, grp * 8:(grp + 1) * 8].rearrange("j p f -> p j f")),
                    reads=["wdT_blk"], sem_key="y", accum=True)
        total_y = b.cnt["d_y"]
        b.q["gpsimd"].append((lambda g: g.wait_ge(b.sems["d_y"], total_y), [], (None, 0)))
        with nc.Block() as block:
            for ename in Builder.ENGINES:
                items = b.q[ename]

                def runner(eng, items=items):
                    waited = {}
                    for fn, waits, (semh, inc) in items:
                        for sname, v in waits:
                            if waited.get(sname, 0) < v:
                                eng.wait_ge(b.sems[sname], v)
                                waited[sname] = v
                        ins = fn(eng)
                        if semh is not None:
                            ins.then_inc(semh, inc)
                getattr(block, ename)(runner)
        ctx.close()
        return nc

    b.alias_release("hh", ["big_sc_init"])
    b.alias_release("gfm", ["big_sc_init"])
    b.alias_release("ofin", ["big_sc_init"])
    b.alias_release("ofm", ["big_sc_init"])

    for s in range(NSLAB):
        xa, xares = (xa_sb, "xa") if s % 2 == 0 else (xb_sb, "xb")
        xb, xbres = (xb_sb, "xb") if s % 2 == 0 else (xa_sb, "xa")
        if s + 1 < NSLAB and DEBUG_TAP != "g0":
            b.dma("sync", x_load(xb, s + 1), writes=[xbres])

        if DEBUG_TAP == "x":
            for c in range(NCH):
                b.op("vector", lambda v, xa=xa, c=c: v.tensor_copy(
                    x1_sb[:, c * DM:(c + 1) * DM], xa[:, c * DM:(c + 1) * DM]),
                    reads=[xares], writes=["x1"])
            ydst = y_out[s * SLAB:(s + 1) * SLAB].rearrange("(c p) f -> p c f", p=128)
            b.dma("gpsimd", lambda g, ydst=ydst: g.dma_start(out=ydst, in_=x1_sb[:]),
                 reads=["x1"], sem_key="y", accum=True)
            continue
        if DEBUG_TAP in ("ln_disabled",):
            emit_ln(xa, xares, f"ln1_{s}")
            if DEBUG_TAP == "ln2":
                b.op("vector", lambda v: v.tensor_copy(x1_sb[:, 0:32], stats_sb[:]),
                     reads=["st_sums", "st_mean", "st_msq", "st_var", "st_lnv",
                            "st_rstd", "st_nb"], writes=["x1"])
                for c in range(1, NCH):
                    b.op("scalar", lambda sc, xa=xa, c=c: sc.activation(
                        x1_sb[:, c * DM:(c + 1) * DM], xa[:, c * DM:(c + 1) * DM],
                        AF.Identity, bias=NB[:, c:c + 1], scale=RSTD[:, c:c + 1]),
                        reads=[xares, "st_rstd", "st_nb"], writes=["x1"])
            elif DEBUG_TAP == "st":
                b.op("vector", lambda v: v.tensor_copy(x1_sb[:, 0:32], stats_sb[:]),
                     reads=["st_sums", "st_mean", "st_msq", "st_var", "st_lnv",
                            "st_rstd", "st_nb"], writes=["x1"])
                b.op("vector", lambda v: v.tensor_copy(x1_sb[:, 32:36], consts_sb[:]),
                     reads=["consts"], writes=["x1"])
            elif DEBUG_TAP == "ln":
                for c in range(NCH):
                    b.op("scalar", lambda sc, xa=xa, c=c: sc.activation(
                        x1_sb[:, c * DM:(c + 1) * DM], xa[:, c * DM:(c + 1) * DM],
                        AF.Identity, bias=NB[:, c:c + 1], scale=RSTD[:, c:c + 1]),
                        reads=[xares, "st_rstd", "st_nb"], writes=["x1"])
            else:
                for c in range(NCH):
                    def ytr(t, c=c):
                        for i in range(8):
                            ins = t.transpose(pT[:, i * 128:(i + 1) * 128],
                                              fm_sc[:, i * 512 + c * 128:i * 512 + (c + 1) * 128],
                                              ident_sb[:])
                        return ins
                    b.op("tensor", ytr, reads=["fm", "ident"], writes=["pT"])
                    b.op("vector", lambda v, c=c: v.tensor_copy(
                        x1_sb[:, c * DM:(c + 1) * DM], pT[:]),
                        reads=["pT"], writes=["x1"])
            ydst = y_out[s * SLAB:(s + 1) * SLAB].rearrange("(c p) f -> p c f", p=128)
            b.dma("gpsimd", lambda g, ydst=ydst: g.dma_start(out=ydst, in_=x1_sb[:]),
                 reads=["x1"], sem_key="y", accum=True)
            continue

        if s == 0:
            emit_ln_stats(xa, xares, 1)
        emit_ln_apply(xa, xares, 1)
        emit_ffn(xa, xares, x1_sb, "x1", 0)
        if DEBUG_TAP == "g0":
            ydst2 = y_out[0:512].rearrange("(c p) f -> p c f", p=128)
            b.dma("gpsimd", lambda g, ydst2=ydst2: g.dma_start(out=ydst2, in_=xb_sb[:]),
                  reads=["xb"], sem_key="y", accum=True)
            break
        if DEBUG_TAP == "hh":
            for m in range(24):
                b.dma("gpsimd", lambda g, m=m: g.dma_start(
                    out=y_out[m * 128:(m + 1) * 128, 0:512], in_=HH(m)),
                    reads=["hh"], sem_key="y", accum=True)
            ydst2 = y_out[3584:4096].rearrange("(c p) f -> p c f", p=128)
            b.dma("gpsimd", lambda g, ydst2=ydst2: g.dma_start(out=ydst2, in_=x1_sb[:]),
                  reads=["x1"], sem_key="y", accum=True)
            break
        if N_SUBLAYERS >= 2:
            b.alias_release("gfm", ["hh"])
            b.alias_release("ofin", ["hh"])
            b.alias_release("ofm", ["hh"])
            emit_ln_stats(x1_sb, "x1", 0, have_sums=True, have_sumsq=True)
            emit_ln_apply(x1_sb, "x1", 0)
            emit_attn(x1_sb, "x1", xa, xares, s)
            if DEBUG_TAP == "of":
                ydst3 = y_out[0:512].rearrange("(c p) f -> p c f", p=128)
                b.dma("gpsimd", lambda g, ydst3=ydst3: g.dma_start(
                    out=ydst3, in_=OFIN.bitcast(F32) if False else OFIN),
                    reads=["ofin"], sem_key="y", accum=True)
                ydst4 = y_out[512:1024].rearrange("(c p) f -> p c f", p=128)
                b.dma("gpsimd", lambda g, ydst4=ydst4: g.dma_start(out=ydst4, in_=GFM),
                      reads=["gfm"], sem_key="y", accum=True)
                ydst5 = y_out[1024:1536].rearrange("(c p) f -> p c f", p=128)
                b.dma("gpsimd", lambda g, ydst5=ydst5: g.dma_start(out=ydst5, in_=qfm_sb[:]),
                      reads=["qfm"], sem_key="y", accum=True)
                ydst6 = y_out[1536:2048].rearrange("(c p) f -> p c f", p=128)
                b.dma("gpsimd", lambda g, ydst6=ydst6: g.dma_start(out=ydst6, in_=kfm_sb[:]),
                      reads=["kfm"], sem_key="y", accum=True)
                break
        if N_SUBLAYERS >= 3:
            b.alias_release("hh", ["gfm", "ofin", "ofm"])
            emit_ln_stats(xa, xares, 0, have_sums=True, have_sumsq=True)
            emit_ln_apply(xa, xares, 0)
            if s + 1 < NSLAB:
                xnb, xnres = (xb_sb, "xb") if s % 2 == 0 else (xa_sb, "xa")
                emit_ln_stats(xnb, xnres, 1)
            emit_ffn(xa, xares, x1_sb, "x1", 1)
        outres = "x1" if N_SUBLAYERS != 2 else xares
        outbuf = x1_sb if N_SUBLAYERS != 2 else xa
        ydst = y_out[s * SLAB:(s + 1) * SLAB].rearrange("(c p) f -> p c f", p=128)
        b.dma("gpsimd", lambda g, outbuf=outbuf, ydst=ydst: g.dma_start(out=ydst, in_=outbuf[:]),
             reads=[outres], sem_key="y", accum=True)

    total_y = b.cnt["d_y"]
    b.q["gpsimd"].append((lambda g: g.wait_ge(b.sems["d_y"], total_y), [], (None, 0)))

    with nc.Block() as block:
        for ename in Builder.ENGINES:
            items = b.q[ename]

            def runner(eng, items=items):
                waited = {}
                for fn, waits, (semh, inc) in items:
                    for sname, v in waits:
                        if waited.get(sname, 0) < v:
                            eng.wait_ge(b.sems[sname], v)
                            waited[sname] = v
                    ins = fn(eng)
                    if semh is not None:
                        ins.then_inc(semh, inc)
            getattr(block, ename)(runner)

    ctx.close()
    return nc


_NC_CACHE = {}


def _get_nc():
    if "nc" not in _NC_CACHE:
        _NC_CACHE["nc"] = build_program()
    return _NC_CACHE["nc"]


def kernel(L0_attn_x, L0_attn_w_qkv, L0_attn_w_gate, L0_attn_w_out,
           L0_ffn_w_gate, L0_ffn_w_up, L0_ffn_w_down):
    x = np.ascontiguousarray(np.asarray(L0_attn_x, dtype=np.float32))
    w_qkv = np.ascontiguousarray(np.asarray(L0_attn_w_qkv, dtype=np.float32))
    w_gate = np.ascontiguousarray(np.asarray(L0_attn_w_gate, dtype=np.float32))
    w_out = np.ascontiguousarray(np.asarray(L0_attn_w_out, dtype=np.float32))
    wg = np.ascontiguousarray(np.asarray(L0_ffn_w_gate, dtype=np.int32))
    wu = np.ascontiguousarray(np.asarray(L0_ffn_w_up, dtype=np.int32))
    wd = np.ascontiguousarray(np.asarray(L0_ffn_w_down, dtype=np.int32))

    ident = np.eye(128, dtype=ml_dtypes.bfloat16)
    mask4 = np.zeros((128, 512), dtype=np.float32)
    tri = (np.arange(128)[:, None] <= np.arange(128)[None, :]).astype(np.float32)
    for j in range(4):
        mask4[:, j * 128:(j + 1) * 128] = tri
    consts = np.tile(np.array([[EPS_LN, -0.5, SFF, 1.0]], dtype=np.float32), (128, 1))
    zeros = np.zeros((128, 512), dtype=ml_dtypes.bfloat16)

    nc = _get_nc()
    in_maps = []
    for c in range(8):
        shard = np.ascontiguousarray(x[2 * c:2 * c + 2].reshape(TCORE, DM))
        in_maps.append({
            "x": shard, "w_qkv": w_qkv, "w_gate": w_gate, "w_out": w_out,
            "wg_int": wg, "wu_int": wu, "wd_int": wd,
            "ident": ident, "mask4": mask4, "consts": consts, "zeros": zeros,
        })
    res = run_bass_kernel_spmd(nc, in_maps, core_ids=list(range(8)))
    out = np.empty((16, 2048, DM), dtype=np.float32)
    for c in range(8):
        out[2 * c:2 * c + 2] = np.asarray(res.results[c]["y"]).reshape(2, 2048, DM)
    return out


if __name__ == "__main__":
    import reference as _ref
    inputs = _ref.setup_inputs()
    out = kernel(**{k: np.asarray(v) for k, v in inputs.items()})
    exp = np.asarray(_ref.reference(**inputs))
    err = np.linalg.norm(out - exp) / np.linalg.norm(exp)
    print("rel_err:", err)
